# version 10
# baseline (speedup 1.0000x reference)
import numpy as np
import ml_dtypes
import concourse.bass as bass
import concourse.mybir as mybir
from concourse.bass_utils import run_bass_kernel_spmd

F32 = mybir.dt.float32
BF16 = mybir.dt.bfloat16
AX = mybir.AxisListType.X
ALU = mybir.AluOpType
ACT = mybir.ActivationFunctionType

D = 1024
H = 2048
S = 8192
NT = 512
NSUB = S // NT
NOWN = 8
DC = D // 128
HC = H // 128
HQ = 4
MQ = HC // HQ
F = 768
FC = F // 128
NE = 9
E = 8
TT128 = NT // 128

ENGS = ("sync", "gpsimd", "vector", "scalar", "tensor")


class Rec:

    def __init__(self):
        self.ops = []
        self.ctr = {}
        self.dma_sems = set()

    def op(self, eng, fn, waits=(), inc=None):
        tok = None
        if inc is not None:
            sem, n = inc
            self.ctr[sem] = self.ctr.get(sem, 0) + n
            tok = (sem, self.ctr[sem])
        self.ops.append((eng, fn, [w for w in waits if w is not None], inc))
        return tok

    def emit(self, block, sems):
        seen = {e: {} for e in ENGS}

        def section(eng_name):
            def run(engobj):
                sn = seen[eng_name]
                for (eng, fn, waits, inc) in self.ops:
                    if eng != eng_name:
                        continue
                    for (sem, val) in sorted(waits):
                        if sn.get(sem, 0) < val:
                            engobj.wait_ge(sems[sem], val)
                            sn[sem] = val
                    res = fn(engobj) if fn is not None else None
                    if inc is not None:
                        sem, n = inc
                        insts = res if isinstance(res, (list, tuple)) else [res]
                        if sem in self.dma_sems:
                            assert n == 16 * len(insts)
                            for i in insts:
                                i.then_inc(sems[sem], 16)
                        else:
                            insts[-1].then_inc(sems[sem], n)
            return run

        block.sync(section("sync"))
        block.gpsimd(section("gpsimd"))
        block.vector(section("vector"))
        block.scalar(section("scalar"))
        block.tensor(section("tensor"))


class Buf:

    def __init__(self, mem, key):
        self.mem = mem
        self.key = key
        self.wtok = None
        self.readers = []

    def overlaps(self, other):
        a, s0, e0, b0 = self.key
        a2, s1, e1, b1 = other.key
        if a != a2:
            return False
        return s0 * b0 < e1 * b1 and s1 * b1 < e0 * b0


class Mem:
    def __init__(self):
        self.bufs = []

    def buf(self, arena_id, start, end, elem_bytes=1):
        b = Buf(self, (arena_id, start, end, elem_bytes))
        self.bufs.append(b)
        return b

    def waits_for_write(self, out):
        w = []
        for b in self.bufs:
            if b.overlaps(out):
                w += b.readers
                if b.wtok is not None:
                    w.append(b.wtok)
        return w

    def mark_write(self, out, tok):
        for b in self.bufs:
            if b.overlaps(out):
                b.readers = []
                b.wtok = None
        out.wtok = tok
        out.readers = []


def build_program():
    nc = bass.Bass("TRN2", target_bir_lowering=False, debug=False,
                   detect_race_conditions=False)

    xc = nc.dram_tensor("xc", [D, S], F32, kind="ExternalInput")
    wk_d = nc.dram_tensor("wk_d", [D, H], F32, kind="ExternalInput")
    wugg_d = nc.dram_tensor("wugg_d", [D, 3 * H], F32, kind="ExternalInput")
    wout_d = nc.dram_tensor("wout_d", [H, D], F32, kind="ExternalInput")
    wr_d = nc.dram_tensor("wr_d", [D, E], F32, kind="ExternalInput")
    biasr_d = nc.dram_tensor("biasr_d", [1, E], F32, kind="ExternalInput")
    wg_d = nc.dram_tensor("wg_d", [NE * D, F], F32, kind="ExternalInput")
    wu_d = nc.dram_tensor("wu_d", [NE * D, F], F32, kind="ExternalInput")
    wd_d = nc.dram_tensor("wd_d", [NE * F, D], F32, kind="ExternalInput")
    rms_d = nc.dram_tensor("rms_d", [128, DC], F32, kind="ExternalInput")
    split_d = nc.dram_tensor("split_d", [128, DC], F32, kind="ExternalInput")
    ident_d = nc.dram_tensor("ident_d", [128, 128], F32, kind="ExternalInput")
    onehot_d = nc.dram_tensor("onehot_d", [E, E * 128], F32, kind="ExternalInput")

    o_core = nc.dram_tensor("o_core", [D, NOWN * NT], F32, kind="ExternalOutput")

    wk_bf = nc.dram_tensor("wk_bf", [D, H], BF16, kind="Internal")
    wugg_bf = nc.dram_tensor("wugg_bf", [D, 3 * H], BF16, kind="Internal")
    wout_bf = nc.dram_tensor("wout_bf", [H, D], BF16, kind="Internal")
    wg_bf = nc.dram_tensor("wg_bf", [NE * D, F], BF16, kind="Internal")
    wu_bf = nc.dram_tensor("wu_bf", [NE * D, F], BF16, kind="Internal")
    wd_bf = nc.dram_tensor("wd_bf", [NE * F, D], BF16, kind="Internal")
    y_dram = nc.dram_tensor("y_dram", [D, NOWN * NT], BF16, kind="Internal")

    xt = nc.alloc_sbuf_tensor("xt", [128, DC * NT], F32)
    state = nc.alloc_sbuf_tensor("state", [128, HC], F32)
    kprev = nc.alloc_sbuf_tensor("kprev", [128, HC], F32)
    scvec = nc.alloc_sbuf_tensor("scvec", [128, DC], F32)
    rms_t = nc.alloc_sbuf_tensor("rms_t", [128, DC], F32)
    rows = nc.alloc_sbuf_tensor("rows", [1, 4 * NT], F32)
    rows2 = nc.alloc_sbuf_tensor("rows2", [1, 4 * NT], F32)
    rows2_bf = nc.alloc_sbuf_tensor("rows2_bf", [1, 2 * NT], BF16)
    sqs = nc.alloc_sbuf_tensor("sqs", [128, 2 * NT], BF16)
    ones1 = nc.alloc_sbuf_tensor("ones1", [128, 1], F32)
    onesk1 = nc.alloc_sbuf_tensor("onesk1", [1, 128], F32)
    ones1b = nc.alloc_sbuf_tensor("ones1b", [128, 1], BF16)
    onesk1b = nc.alloc_sbuf_tensor("onesk1b", [1, 128], BF16)
    rows_bf = nc.alloc_sbuf_tensor("rows_bf", [1, 2 * NT], BF16)
    biasr = nc.alloc_sbuf_tensor("biasr", [1, E], F32)
    biasB = nc.alloc_sbuf_tensor("biasB", [128, E], F32)
    ident = nc.alloc_sbuf_tensor("ident", [128, 128], F32)
    onehot = nc.alloc_sbuf_tensor("onehot", [E, E * 128], F32)
    wr_f = nc.alloc_sbuf_tensor("wr_f", [128, DC * E], F32)
    wr_b = nc.alloc_sbuf_tensor("wr_b", [128, DC * E], BF16)
    cst = nc.alloc_sbuf_tensor("cst", [128, 6], F32)
    sctok = nc.alloc_sbuf_tensor("sctok", [128, TT128 * E], F32)
    wtok = nc.alloc_sbuf_tensor("wtok", [128, TT128 * E], F32)
    scr = nc.alloc_sbuf_tensor("scr", [128, 2 * E], F32)
    col = nc.alloc_sbuf_tensor("col", [128, 4], F32)
    cden = nc.alloc_sbuf_tensor("cden", [128, 2 * TT128], F32)
    wT = nc.alloc_sbuf_tensor("wT", [E, NT], F32)
    wsb = nc.alloc_sbuf_tensor("wsb", [128, NT], F32)
    t1 = nc.alloc_sbuf_tensor("t1", [128, NT], F32)
    t2 = nc.alloc_sbuf_tensor("t2", [128, NT], F32)

    for i, v in enumerate([1e-5, -0.5, 0.5, -1.0, 1e-30]):
        nc.const_aps.register(F32, v, cst[:, i:i + 1])

    AF = nc.alloc_sbuf_tensor("AF", [128, 12296], F32)
    AB = nc.alloc_sbuf_tensor("AB", [128, 53248], BF16)

    pb = [nc.alloc_psum_tensor(f"pb{i}", [128, NT], F32) for i in range(8)]

    mem = Mem()
    R = Rec()

    def af(start, n, name=None):
        return AF[:, start:start + n], mem.buf("AF", start, start + n, 4)

    def ab(start, n, name=None):
        return AB[:, start:start + n], mem.buf("AB", start, start + n, 2)

    def sbuf_whole(t, aid):
        n = int(np.prod(t.shape[1:]))
        return mem.buf(aid, 0, n, 4)

    b_xt = sbuf_whole(xt, "xt")
    b_state = sbuf_whole(state, "state")
    b_kprev = sbuf_whole(kprev, "kprev")
    b_rows = sbuf_whole(rows, "rows")
    b_sc = sbuf_whole(sctok, "sctok")
    b_sqs = [mem.buf("sqs", i * NT, (i + 1) * NT, 2) for i in range(2)]
    b_rowsbf = mem.buf("rows_bf", 0, 2 * NT, 2)
    b_rows2 = mem.buf("rows2", 0, 4 * NT, 4)
    b_rows2bf = mem.buf("rows2_bf", 0, 2 * NT, 2)
    b_wtok = sbuf_whole(wtok, "wtok")
    b_scr = sbuf_whole(scr, "scr")
    b_col = sbuf_whole(col, "col")
    b_cden = sbuf_whole(cden, "cden")
    b_wT = sbuf_whole(wT, "wT")
    b_wsb = sbuf_whole(wsb, "wsb")
    b_t1 = sbuf_whole(t1, "t1")
    b_t2 = sbuf_whole(t2, "t2")
    b_pb = [mem.buf(f"pb{i}", 0, NT, 4) for i in range(8)]
    b_ydram = mem.buf("y_dram", 0, D * NOWN * NT, 2)
    b_woutdram = mem.buf("wout_bf_dram", 0, H * D, 2)
    b_small = mem.buf("smallconsts", 0, 1, 4)

    KQW = NT + 1
    Kq = AF[:, 0:MQ * KQW]
    b_Kq = mem.buf("AF", 0, MQ * KQW, 4)
    sgq, b_sgq = af(2056, MQ * NT)
    uq, b_uq = af(4104, MQ * NT)
    u3q, b_u3q = af(6152, MQ * NT)
    soq, b_soq = af(8200, MQ * NT)
    sgoq, b_sgoq = af(10248, MQ * NT)
    zf, b_zf = af(0, DC * NT)
    x1f, b_x1f = af(4096, DC * NT)
    accf, b_acc = af(8192, DC * NT)
    outf, b_outf = af(8192, DC * NT)
    sqf, b_sqf = af(8192, DC * NT)
    wnegS = [af(0, 8 * 1024), af(8192, 4096)]


    xnbS = [ab(0, DC * NT), ab(49152, DC * NT)]
    og, b_og = ab(4096, HC * NT)
    wobS = [ab(12288 + i * HC * 128, HC * 128) for i in range(2)]
    wkqS = [ab(16384 + i * DC * NT, DC * NT) for i in range(2)]
    wuqS = [ab(24576 + i * DC * NT, DC * NT) for i in range(2)]
    wgiqS = [ab(32768 + i * DC * NT, DC * NT) for i in range(2)]
    wgoqS = [ab(40960 + i * DC * NT, DC * NT) for i in range(2)]
    ystage, b_ystage = ab(16384, DC * NT)
    ytile, b_ytile = ab(49152, DC * NT)
    x1b, b_x1b = ab(0, DC * NT)
    hS = [ab(4096 + i * FC * NT, FC * NT) for i in range(2)]
    wgS = [ab(10240 + i * DC * F, DC * F) for i in range(2)]
    wuS = [ab(22528 + i * DC * F, DC * F) for i in range(2)]
    wdS = [ab(34816 + i * FC * 1024, FC * 1024) for i in range(2)]
    sqfb, b_sqfb = ab(49152, DC * NT)
    wnegb, b_wnegb = ab(0, 8 * 1024)

    def emit(eng, fn, ins=(), outs=(), extra=(), sem=None, n=1):
        if sem is None:
            sem = {"sync": "ldma", "gpsimd": "gps", "vector": "v",
                   "scalar": "s", "tensor": "t"}[eng]
        waits = []
        for b in ins:
            if b.wtok is not None:
                waits.append(b.wtok)
        for ob in outs:
            waits += mem.waits_for_write(ob)
        waits += [w for w in extra if w is not None]
        if eng in ("sync",) or (eng == "gpsimd" and n % 16 == 0 and n >= 16):
            R.dma_sems.add(sem)
        tok = R.op(eng, fn, waits, inc=(sem, n))
        for b in ins:
            b.readers.append(tok)
        for ob in outs:
            mem.mark_write(ob, tok)
        return tok

    emit("gpsimd", lambda g: [
        g.memset(ones1[:, :], 1.0),
        g.memset(onesk1[:, :], 1.0),
        g.memset(ones1b[:, :], 1.0),
        g.memset(onesk1b[:, :], 1.0),
        g.memset(state[:, :], 0.0),
        g.memset(kprev[:, :], 0.0),
        g.memset(cst[:, 0:1], 1e-5),
        g.memset(cst[:, 1:2], -0.5),
        g.memset(cst[:, 2:3], 0.5),
        g.memset(cst[:, 3:4], -1.0),
        g.memset(cst[:, 4:5], 1e-30),
    ][-1], outs=[b_state, b_kprev], sem="gps")
    tk_memset = R.ctr["gps"]

    tok_casts_m = emit("gpsimd", lambda g: [
        g.dma_start(wk_bf[:, :], wk_d[:, :]),
        g.dma_start(wugg_bf[:, :], wugg_d[:, :]),
    ], sem="gcm", n=16 * 2)
    tok_casts = emit("gpsimd", lambda g: [
        g.dma_start(wg_bf[:, :], wg_d[:, :]),
        g.dma_start(wu_bf[:, :], wu_d[:, :]),
        g.dma_start(wd_bf[:, :], wd_d[:, :]),
    ], sem="gcf", n=16 * 3)
    tok_small = emit("sync", lambda sy: [
        sy.dma_start(rms_t[:, :], rms_d[:, :]),
        sy.dma_start(scvec[:, :], split_d[:, :]),
        sy.dma_start(biasr[:, :], biasr_d[:, :]),
        sy.dma_start(ident[:, :], ident_d[:, :]),
        sy.dma_start(onehot[:, :], onehot_d[:, :]),
        sy.dma_start(wr_f.ap().rearrange("p (c e) -> p c e", e=E),
                     wr_d.ap().rearrange("(c p) e -> p c e", p=128)),
    ], outs=[b_small], sem="lsm", n=16 * 6)
    emit("vector", lambda v: [
        v.tensor_mul(scvec[:, :], scvec[:, :], rms_t[:, :]),
        v.tensor_copy(wr_b[:, :], wr_f[:, :]),
    ][-1], ins=[b_small], outs=[b_small], extra=[tok_small])
    tok_scvec = ("v", R.ctr["v"])
    tok_bias_mm = emit("tensor", lambda t: t.matmul(
        pb[0][:, 0:E], onesk1[0:1, :], biasr[0:1, :], start=True, stop=True),
        ins=[b_small], outs=[b_pb[0]], extra=[tok_small, (
            "gps", tk_memset)])
    emit("vector", lambda v: v.tensor_copy(biasB[:, :], pb[0][:, 0:E]),
         ins=[b_pb[0]], outs=[b_small])

    def record_ln_stats(src, b_src, sqbuf, b_sq, rw, b_rw, rwb, b_rwb):
        tok_sq = emit("gpsimd", lambda g: [
            g.tensor_mul(sqbuf[:, c * NT:(c + 1) * NT], src[:, c * NT:(c + 1) * NT],
                         src[:, c * NT:(c + 1) * NT])
            for c in range(DC)][-1], ins=[b_src], outs=[b_sq])
        tok_s1 = emit("tensor", lambda t: [
            t.matmul(pb[0][0:1, :], ones1[:, 0:1], src[:, c * NT:(c + 1) * NT],
                     start=(c == 0), stop=(c == DC - 1)) for c in range(DC)][-1],
            ins=[b_src, b_small], outs=[b_pb[0]])
        tok_s2 = emit("tensor", lambda t: [
            t.matmul(pb[1][0:1, :], ones1b[:, 0:1], sqbuf[:, c * NT:(c + 1) * NT],
                     start=(c == 0), stop=(c == DC - 1)) for c in range(DC)][-1],
            ins=[b_sq, b_small], outs=[b_pb[1]], extra=[tok_sq])
        emit("scalar", lambda s: [
            s.copy(rw[0:1, 0:NT], pb[0][0:1, :]),
            s.copy(rw[0:1, NT:2 * NT], pb[1][0:1, :]),
        ][-1], ins=[b_pb[0], b_pb[1]], outs=[b_rw], extra=[tok_s1, tok_s2])
        emit("vector", lambda v: v.tensor_scalar(
            rw[0:1, 2 * NT:3 * NT], rw[0:1, 0:NT], 1.0 / D, None, ALU.mult),
            ins=[b_rw], outs=[b_rw])
        emit("vector", lambda v: v.tensor_scalar(
            rw[0:1, 3 * NT:4 * NT], rw[0:1, NT:2 * NT], 1.0 / D, None, ALU.mult),
            ins=[b_rw], outs=[b_rw])
        emit("vector", lambda v: v.tensor_mul(
            rw[0:1, NT:2 * NT], rw[0:1, 2 * NT:3 * NT], rw[0:1, 2 * NT:3 * NT]),
            ins=[b_rw], outs=[b_rw])
        emit("vector", lambda v: v.tensor_sub(
            rw[0:1, 3 * NT:4 * NT], rw[0:1, 3 * NT:4 * NT], rw[0:1, NT:2 * NT]),
            ins=[b_rw], outs=[b_rw])
        emit("scalar", lambda s: s.activation(
            rw[0:1, NT:2 * NT], rw[0:1, 3 * NT:4 * NT], ACT.Ln, bias=1e-5),
            ins=[b_rw], outs=[b_rw])
        emit("scalar", lambda s: s.activation(
            rw[0:1, 3 * NT:4 * NT], rw[0:1, NT:2 * NT], ACT.Exp, scale=-0.5),
            ins=[b_rw], outs=[b_rw])
        emit("vector", lambda v: v.tensor_mul(
            rw[0:1, NT:2 * NT], rw[0:1, 2 * NT:3 * NT], rw[0:1, 3 * NT:4 * NT]),
            ins=[b_rw], outs=[b_rw])
        emit("vector", lambda v: [
            v.tensor_copy(rwb[0:1, 0:NT], rw[0:1, 3 * NT:4 * NT]),
            v.tensor_copy(rwb[0:1, NT:2 * NT], rw[0:1, NT:2 * NT]),
        ][-1], ins=[b_rw], outs=[b_rwb])

    def record_ln_apply(src, b_src, out_specs, rwb, b_rwb):
        emit("tensor", lambda t: t.matmul(
            pb[2][:, :], onesk1b[0:1, :], rwb[0:1, 0:NT],
            start=True, stop=True), ins=[b_rwb, b_small], outs=[b_pb[2]])
        emit("tensor", lambda t: t.matmul(
            pb[3][:, :], onesk1b[0:1, :], rwb[0:1, NT:2 * NT],
            start=True, stop=True), ins=[b_rwb, b_small], outs=[b_pb[3]])
        d0, bd0 = out_specs[0]
        emit("vector", lambda v: [
            v.tensor_mul(d0[:, c * NT:(c + 1) * NT], src[:, c * NT:(c + 1) * NT],
                         pb[2][:, :]) for c in range(DC)][-1],
            ins=[b_src, b_pb[2]], outs=[bd0])
        emit("vector", lambda v: [
            v.tensor_sub(d0[:, c * NT:(c + 1) * NT], d0[:, c * NT:(c + 1) * NT],
                         pb[3][:, :]) for c in range(DC)][-1],
            ins=[b_pb[3]], outs=[bd0])
        for (dd, bdd) in out_specs[1:]:
            emit("vector", lambda v, dd=dd: [
                v.tensor_copy(dd[:, c * NT:(c + 1) * NT], d0[:, c * NT:(c + 1) * NT])
                for c in range(DC)][-1], ins=[bd0], outs=[bdd])

    def emit_xt_load(st_i):
        T0 = st_i * NT
        emit("sync", lambda sy, T0=T0: sy.dma_start(
            xt.ap().rearrange("p (c n) -> p c n", n=NT),
            xc.ap()[:, T0:T0 + NT].rearrange("(c p) n -> p c n", p=128)),
            outs=[b_xt], sem="ldx", n=16)

    def front_end(st_i):
        xnb, b_xnb = xnbS[st_i % 2]
        for c in range(DC):
            sl, b_sl = sqs[:, (c % 2) * NT:(c % 2 + 1) * NT], b_sqs[c % 2]
            emit("gpsimd", lambda g, sl=sl, c=c: g.tensor_mul(
                sl, xt[:, c * NT:(c + 1) * NT], xt[:, c * NT:(c + 1) * NT]),
                ins=[b_xt], outs=[b_sl])
            emit("tensor", lambda t, sl=sl, c=c: t.matmul(
                pb[0][0:1, :], ones1b[:, 0:1], sl,
                start=(c == 0), stop=(c == DC - 1)),
                ins=[b_sl, b_small], outs=[b_pb[0]])
        emit("scalar", lambda s: s.copy(rows[0:1, 0:NT], pb[0][0:1, :]),
             ins=[b_pb[0]], outs=[b_rows])
        emit("scalar", lambda s: s.activation(
            rows[0:1, NT:2 * NT], rows[0:1, 0:NT], ACT.Ln, bias=1e-30),
            ins=[b_rows], outs=[b_rows])
        emit("scalar", lambda s: s.activation(
            rows[0:1, 2 * NT:3 * NT], rows[0:1, NT:2 * NT], ACT.Exp, scale=0.5),
            ins=[b_rows], outs=[b_rows])
        emit("vector", lambda v: v.tensor_scalar(
            rows[0:1, 3 * NT:4 * NT], rows[0:1, 2 * NT:3 * NT], 1.0 / 32.0, 1e-8,
            ALU.mult, ALU.add), ins=[b_rows], outs=[b_rows])
        emit("scalar", lambda s: s.activation(
            rows[0:1, NT:2 * NT], rows[0:1, 3 * NT:4 * NT], ACT.Ln),
            ins=[b_rows], outs=[b_rows])
        emit("scalar", lambda s: s.activation(
            rows[0:1, 2 * NT:3 * NT], rows[0:1, NT:2 * NT], ACT.Exp, scale=-1.0),
            ins=[b_rows], outs=[b_rows])
        emit("vector", lambda v: v.tensor_copy(
            rows_bf[0:1, 0:NT], rows[0:1, 2 * NT:3 * NT]),
            ins=[b_rows], outs=[b_rowsbf])
        emit("tensor", lambda t: t.matmul(
            pb[1][:, :], onesk1b[0:1, :], rows_bf[0:1, 0:NT],
            start=True, stop=True), ins=[b_rowsbf, b_small], outs=[b_pb[1]])
        emit("vector", lambda v, xnb=xnb: [
            v.scalar_tensor_tensor(xnb[:, c * NT:(c + 1) * NT],
                                   xt[:, c * NT:(c + 1) * NT],
                                   scvec[:, c:c + 1], pb[1][:, :],
                                   ALU.mult, ALU.mult) for c in range(DC)][-1],
            ins=[b_xt, b_pb[1], b_small], outs=[b_xnb], extra=[tok_scvec])

    def quarters(st_i, qrange, xnb, b_xnb, owned):
        for q in qrange:
            qs = q * NT
            par = (st_i * HQ + q) % 2
            (wkq, b_wkq) = wkqS[par]
            (wuq, b_wuq) = wuqS[par]
            (wgiq, b_wgiq) = wgiqS[par]
            (wgoq, b_wgoq) = wgoqS[par]
            if st_i == 0:
                specs = [(wkq, b_wkq, wk_d, qs, "gqk"), (wuq, b_wuq, wugg_d, qs, "gqu"),
                         (wgiq, b_wgiq, wugg_d, H + qs, "gqg")]
            else:
                specs = [(wkq, b_wkq, wk_bf, qs, "lmk"), (wuq, b_wuq, wugg_bf, qs, "lmu"),
                         (wgiq, b_wgiq, wugg_bf, H + qs, "lmg")]
                if owned:
                    specs.append((wgoq, b_wgoq, wugg_bf, 2 * H + qs, "lmo"))
            toks = []
            for (dst, bdst, srcd, c0, sm) in specs:
                eng = "gpsimd" if st_i == 0 else "sync"
                extra = [] if st_i == 0 else [tok_casts_m]
                toks.append(emit(
                    eng,
                    lambda sy, dst=dst, srcd=srcd, c0=c0: sy.dma_start(
                        dst.rearrange("p (c m) -> p c m", m=NT),
                        srcd.ap()[:, c0:c0 + NT]
                        .rearrange("(c p) m -> p c m", p=128)),
                    outs=[bdst], sem=sm + str(par), n=16, extra=extra))
            for m in range(MQ):
                mc = q * MQ + m
                ia = [4, 5, 2, 3][m % 4]
                pA = pb[ia]
                bA = b_pb[ia]
                pB = pb[6 + (m % 2)]
                bB = b_pb[6 + (m % 2)]
                emit("tensor", lambda t, pA=pA, m=m, wkq=wkq, xnb=xnb: [
                    t.matmul(pA[:, :], wkq[:, kc * NT + m * 128: kc * NT + m * 128 + 128],
                             xnb[:, kc * NT:(kc + 1) * NT],
                             start=(kc == 0), stop=(kc == DC - 1))
                    for kc in range(DC)][-1],
                    ins=[b_wkq, b_xnb], outs=[bA], extra=[toks[0]])
                emit("scalar", lambda s, pA=pA, m=m: s.activation(
                    Kq[:, m * KQW + 1: m * KQW + 1 + NT], pA[:, :], ACT.Sigmoid),
                    ins=[bA], outs=[b_Kq])
                emit("vector", lambda v, m=m, mc=mc: v.tensor_copy(
                    Kq[:, m * KQW: m * KQW + 1], kprev[:, mc:mc + 1]),
                    ins=[b_kprev], outs=[b_Kq])
                emit("tensor", lambda t, pB=pB, m=m, wgiq=wgiq, xnb=xnb: [
                    t.matmul(pB[:, :], wgiq[:, kc * NT + m * 128: kc * NT + m * 128 + 128],
                             xnb[:, kc * NT:(kc + 1) * NT],
                             start=(kc == 0), stop=(kc == DC - 1))
                    for kc in range(DC)][-1],
                    ins=[b_wgiq, b_xnb], outs=[bB], extra=[toks[2]])
                emit("scalar", lambda s, pB=pB, m=m: s.activation(
                    sgq[:, m * NT:(m + 1) * NT], pB[:, :], ACT.Sigmoid),
                    ins=[bB], outs=[b_sgq])
                emit("tensor", lambda t, pA=pA, m=m, wuq=wuq, xnb=xnb: [
                    t.matmul(pA[:, :], wuq[:, kc * NT + m * 128: kc * NT + m * 128 + 128],
                             xnb[:, kc * NT:(kc + 1) * NT],
                             start=(kc == 0), stop=(kc == DC - 1))
                    for kc in range(DC)][-1],
                    ins=[b_wuq, b_xnb], outs=[bA], extra=[toks[1]])
                emit("vector", lambda v, pA=pA, m=m: v.tensor_mul(
                    uq[:, m * NT:(m + 1) * NT], pA[:, :], sgq[:, m * NT:(m + 1) * NT]),
                    ins=[bA, b_sgq], outs=[b_uq])
                emit("vector", lambda v, m=m: v.scalar_tensor_tensor(
                    u3q[:, m * NT:(m + 1) * NT],
                    Kq[:, m * KQW + 1: m * KQW + 1 + NT], 1.0,
                    uq[:, m * NT:(m + 1) * NT], ALU.subtract, ALU.mult),
                    ins=[b_Kq, b_uq], outs=[b_u3q])
                emit("vector", lambda v, m=m, mc=mc: v.tensor_tensor_scan(
                    soq[:, m * NT:(m + 1) * NT], Kq[:, m * KQW: m * KQW + NT],
                    u3q[:, m * NT:(m + 1) * NT], state[:, mc:mc + 1],
                    ALU.mult, ALU.add),
                    ins=[b_Kq, b_u3q, b_state], outs=[b_soq])
                emit("vector", lambda v, m=m, mc=mc: v.tensor_copy(
                    state[:, mc:mc + 1], soq[:, m * NT + NT - 1: m * NT + NT]),
                    ins=[b_soq], outs=[b_state])
                emit("vector", lambda v, m=m, mc=mc: v.tensor_copy(
                    kprev[:, mc:mc + 1], Kq[:, m * KQW + NT: m * KQW + NT + 1]),
                    ins=[b_Kq], outs=[b_kprev])
                if owned:
                    emit("tensor", lambda t, pB=pB, m=m, wgoq=wgoq, xnb=xnb: [
                        t.matmul(pB[:, :],
                                 wgoq[:, kc * NT + m * 128: kc * NT + m * 128 + 128],
                                 xnb[:, kc * NT:(kc + 1) * NT],
                                 start=(kc == 0), stop=(kc == DC - 1))
                        for kc in range(DC)][-1],
                        ins=[b_wgoq, b_xnb], outs=[bB], extra=[toks[3]])
                    emit("scalar", lambda s, pB=pB, m=m: s.activation(
                        sgoq[:, m * NT:(m + 1) * NT], pB[:, :], ACT.Sigmoid),
                        ins=[bB], outs=[b_sgoq])
                    emit("vector", lambda v, m=m, mc=mc: v.tensor_mul(
                        og[:, mc * NT:(mc + 1) * NT], soq[:, m * NT:(m + 1) * NT],
                        sgoq[:, m * NT:(m + 1) * NT]),
                        ins=[b_soq, b_sgoq], outs=[b_og])

    emit_xt_load(0)
    negT = []
    for gg in range(8):
        wneg, b_wneg = af((gg % 4) * 2048, 2 * 1024)
        wnb, b_wnb = ab(40960 + (gg % 4) * 2048, 2 * 1024)
        emit("sync", lambda sy, gg=gg, wneg=wneg: sy.dma_start(
            wneg.rearrange("p (k dd) -> p k dd", dd=1024),
            wout_d.ap()[gg * 256:(gg + 1) * 256, :]
            .rearrange("(k p) dd -> p k dd", p=128)),
            outs=[b_wneg], sem="lwn" + str(gg % 4), n=16)
        emit("vector", lambda v, wneg=wneg, wnb=wnb: [
            v.tensor_scalar(wnb[:, i * 1024:(i + 1) * 1024],
                            wneg[:, i * 1024:(i + 1) * 1024], -1.0, None, ALU.mult)
            for i in range(2)][-1], ins=[b_wneg], outs=[b_wnb])
        emit("gpsimd", lambda g, gg=gg, wnb=wnb: g.dma_start(
            wout_bf.ap()[gg * 256:(gg + 1) * 256, :]
            .rearrange("(k p) dd -> p k dd", p=128),
            wnb.rearrange("p (k dd) -> p k dd", dd=1024)),
            ins=[b_wnb], outs=[b_woutdram], sem="gwn" + str(gg % 4), n=16)
    for i in range(4):
        negT.append(("gwn" + str(i), R.ctr["gwn" + str(i)]))
    tok_woutneg = negT
    front_end(0)
    for st_i in range(NSUB):
        owned = st_i >= NSUB - NOWN
        oi = st_i - (NSUB - NOWN)
        xnb, b_xnb = xnbS[st_i % 2]
        quarters(st_i, [0, 1], xnb, b_xnb, owned)
        if st_i + 1 < NSUB:
            emit_xt_load(st_i + 1)
            front_end(st_i + 1)
        quarters(st_i, [2, 3], xnb, b_xnb, owned)
        if owned:
            for dc in range(DC):
                pY = pb[2 + (dc % 2)]
                bY = b_pb[2 + (dc % 2)]
                (wob, b_wob) = wobS[dc % 2]
                tok_wob = emit("sync", lambda sy, dc=dc, wob=wob: sy.dma_start(
                    wob.rearrange("p (k dd) -> p k dd", dd=128),
                    wout_bf.ap()[:, dc * 128:(dc + 1) * 128]
                    .rearrange("(k p) dd -> p k dd", p=128)),
                    outs=[b_wob], sem="lwo" + str(dc % 2), n=16, extra=tok_woutneg)
                emit("tensor", lambda t, pY=pY, wob=wob: [
                    t.matmul(pY[:, :], wob[:, kc * 128:(kc + 1) * 128],
                             og[:, kc * NT:(kc + 1) * NT],
                             start=(kc == 0), stop=(kc == HC - 1))
                    for kc in range(HC)][-1],
                    ins=[b_wob, b_og], outs=[bY], extra=[tok_wob])
                emit("scalar", lambda s, pY=pY, dc=dc: s.copy(
                    ystage[:, dc * NT:(dc + 1) * NT], pY[:, :]),
                    ins=[bY], outs=[b_ystage])
            emit("gpsimd", lambda g, oi=oi: g.dma_start(
                y_dram.ap()[:, oi * NT:(oi + 1) * NT]
                .rearrange("(c p) n -> p c n", p=128),
                ystage.rearrange("p (c n) -> p c n", n=NT)),
                ins=[b_ystage], outs=[b_ydram], sem="gy", n=16)

    def ffn_loads(oi):
        T0 = (NSUB - NOWN + oi) * NT
        emit("sync", lambda sy, T0=T0: sy.dma_start(
            xt.ap().rearrange("p (c n) -> p c n", n=NT),
            xc.ap()[:, T0:T0 + NT].rearrange("(c p) n -> p c n", p=128)),
            outs=[b_xt], sem="ldx", n=16)
        emit("sync", lambda sy, oi=oi: sy.dma_start(
            ytile.rearrange("p (c n) -> p c n", n=NT),
            y_dram.ap()[:, oi * NT:(oi + 1) * NT]
            .rearrange("(c p) n -> p c n", p=128)),
            ins=[b_ydram], outs=[b_ytile], sem="lyt", n=16)

    def ffn_front(oi):
        emit("gpsimd", lambda g: [
            g.tensor_scalar(zf[:, c * NT:(c + 1) * NT],
                            xt[:, c * NT:(c + 1) * NT], 2.0, None, ALU.mult)
            for c in range(DC)][-1], ins=[b_xt], outs=[b_zf])
        emit("gpsimd", lambda g: [
            g.tensor_add(zf[:, c * NT:(c + 1) * NT], zf[:, c * NT:(c + 1) * NT],
                         ytile[:, c * NT:(c + 1) * NT])
            for c in range(DC)][-1], ins=[b_ytile], outs=[b_zf])
        record_ln_stats(zf, b_zf, sqfb, b_sqfb, rows, b_rows, rows_bf, b_rowsbf)

    ffn_loads(0)
    ffn_front(0)
    for oi in range(NOWN):
        record_ln_apply(zf, b_zf, [(x1f, b_x1f), (x1b, b_x1b)], rows_bf, b_rowsbf)

        emit("tensor", lambda t: [
            t.matmul(pb[0][:, tt * E:(tt + 1) * E],
                     x1b[:, c * NT + tt * 128: c * NT + tt * 128 + 128],
                     wr_b[:, c * E:(c + 1) * E],
                     start=(c == 0), stop=(c == DC - 1))
            for tt in range(TT128) for c in range(DC)][-1],
            ins=[b_x1b, b_small], outs=[b_pb[0]])
        emit("scalar", lambda s: s.activation(
            sctok[:, :], pb[0][:, 0:TT128 * E], ACT.Sigmoid),
            ins=[b_pb[0]], outs=[b_sc])
        for tt in range(TT128):
            stile = sctok[:, tt * E:(tt + 1) * E]
            wtile = wtok[:, tt * E:(tt + 1) * E]
            emit("vector", lambda v, stile=stile, wtile=wtile: v.tensor_add(
                wtile, stile, biasB[:, :]), ins=[b_sc, b_small], outs=[b_wtok])
            emit("vector", lambda v, wtile=wtile: v.reduce_max(
                col[:, 0:1], wtile, axis=AX), ins=[b_wtok], outs=[b_col])
            emit("vector", lambda v, wtile=wtile: v.tensor_scalar(
                scr[:, 0:E], wtile, col[:, 0:1], None, ALU.is_equal),
                ins=[b_wtok, b_col], outs=[b_scr])
            emit("vector", lambda v, wtile=wtile: v.scalar_tensor_tensor(
                scr[:, E:2 * E], scr[:, 0:E], -1e30, wtile, ALU.mult, ALU.add),
                ins=[b_scr, b_wtok], outs=[b_scr])
            emit("vector", lambda v: v.reduce_max(
                col[:, 1:2], scr[:, E:2 * E], axis=AX), ins=[b_scr], outs=[b_col])
            emit("vector", lambda v, wtile=wtile: v.tensor_scalar(
                scr[:, 0:E], wtile, col[:, 1:2], None, ALU.is_ge),
                ins=[b_wtok, b_col], outs=[b_scr])
            emit("vector", lambda v, stile=stile, wtile=wtile: v.tensor_mul(
                wtile, scr[:, 0:E], stile), ins=[b_scr, b_sc], outs=[b_wtok])
            emit("vector", lambda v, tt=tt, wtile=wtile: v.reduce_sum(
                cden[:, tt:tt + 1], wtile, axis=AX), ins=[b_wtok], outs=[b_cden])
        emit("vector", lambda v: v.tensor_scalar(
            cden[:, 0:TT128], cden[:, 0:TT128], 1e-20, None, ALU.add),
            ins=[b_cden], outs=[b_cden])
        emit("scalar", lambda s: s.activation(
            cden[:, TT128:2 * TT128], cden[:, 0:TT128], ACT.Ln),
            ins=[b_cden], outs=[b_cden])
        emit("scalar", lambda s: s.activation(
            cden[:, TT128:2 * TT128], cden[:, TT128:2 * TT128], ACT.Exp, scale=-1.0),
            ins=[b_cden], outs=[b_cden])
        emit("vector", lambda v: [
            v.tensor_scalar(wtok[:, tt * E:(tt + 1) * E], wtok[:, tt * E:(tt + 1) * E],
                            cden[:, TT128 + tt:TT128 + tt + 1], None, ALU.mult)
            for tt in range(TT128)][-1], ins=[b_cden], outs=[b_wtok])
        emit("tensor", lambda t: [
            t.transpose(pb[1][0:E, tt * 128:(tt + 1) * 128],
                        wtok[:, tt * E:(tt + 1) * E], ident[:, :])
            for tt in range(TT128)][-1],
            ins=[b_wtok, b_small], outs=[b_pb[1]])
        emit("vector", lambda v: v.tensor_copy(wT[:, :], pb[1][0:E, 0:NT]),
             ins=[b_pb[1]], outs=[b_wT])

        for e in range(NE):
            par = e % 2
            (wgbuf, b_wgb) = wgS[par]
            (wubuf, b_wub) = wuS[par]
            (wdbuf, b_wdb) = wdS[par]
            (hbuf, b_h) = hS[par]
            tok_wg = emit("sync", lambda sy, e=e, wgbuf=wgbuf: sy.dma_start(
                wgbuf.rearrange("p (c f) -> p c f", f=F),
                wg_bf.ap()[e * D:(e + 1) * D, :].rearrange("(c p) f -> p c f", p=128)),
                outs=[b_wgb], sem="lfg" + str(par), n=16, extra=[tok_casts])
            tok_wu = emit("sync", lambda sy, e=e, wubuf=wubuf: sy.dma_start(
                wubuf.rearrange("p (c f) -> p c f", f=F),
                wu_bf.ap()[e * D:(e + 1) * D, :].rearrange("(c p) f -> p c f", p=128)),
                outs=[b_wub], sem="lfu" + str(par), n=16, extra=[tok_casts])
            tok_wd = emit("sync", lambda sy, e=e, wdbuf=wdbuf: sy.dma_start(
                wdbuf.rearrange("p (k dd) -> p k dd", dd=1024),
                wd_bf.ap()[e * F:(e + 1) * F, :].rearrange("(k p) dd -> p k dd", p=128)),
                outs=[b_wdb], sem="lfd" + str(par), n=16, extra=[tok_casts])
            if e < E:
                emit("tensor", lambda t, e=e: t.matmul(
                    pb[2][:, :], onehot[:, e * 128:(e + 1) * 128], wT[:, :],
                    start=True, stop=True), ins=[b_wT, b_small], outs=[b_pb[2]])
                emit("scalar", lambda s: s.copy(wsb[:, :], pb[2][:, :]),
                     ins=[b_pb[2]], outs=[b_wsb])
            for fc in range(FC):
                pA = pb[4 + (fc % 2)]
                pB = pb[6 + (fc % 2)]
                bA = b_pb[4 + (fc % 2)]
                bB = b_pb[6 + (fc % 2)]
                emit("tensor", lambda t, pA=pA, fc=fc, wgbuf=wgbuf: [
                    t.matmul(pA[:, :], wgbuf[:, kc * F + fc * 128: kc * F + fc * 128 + 128],
                             x1b[:, kc * NT:(kc + 1) * NT],
                             start=(kc == 0), stop=(kc == DC - 1))
                    for kc in range(DC)][-1],
                    ins=[b_wgb, b_x1b], outs=[bA], extra=[tok_wg])
                emit("tensor", lambda t, pB=pB, fc=fc, wubuf=wubuf: [
                    t.matmul(pB[:, :], wubuf[:, kc * F + fc * 128: kc * F + fc * 128 + 128],
                             x1b[:, kc * NT:(kc + 1) * NT],
                             start=(kc == 0), stop=(kc == DC - 1))
                    for kc in range(DC)][-1],
                    ins=[b_wub, b_x1b], outs=[bB], extra=[tok_wu])
                emit("scalar", lambda s, pA=pA: s.activation(
                    t1[:, :], pA[:, :], ACT.Silu), ins=[bA], outs=[b_t1])
                hslot = hbuf[:, fc * NT:(fc + 1) * NT]
                if e < E:
                    emit("vector", lambda v, pB=pB: v.tensor_mul(
                        t2[:, :], t1[:, :], pB[:, :]), ins=[b_t1, bB], outs=[b_t2])
                    emit("vector", lambda v, hslot=hslot: v.tensor_mul(
                        hslot, t2[:, :], wsb[:, :]),
                        ins=[b_t2, b_wsb], outs=[b_h])
                else:
                    emit("vector", lambda v, pB=pB, hslot=hslot: v.tensor_mul(
                        hslot, t1[:, :], pB[:, :]), ins=[b_t1, bB], outs=[b_h])
            if e == 0 and oi + 1 < NOWN:
                ffn_loads(oi + 1)
            if e == 3 and oi + 1 < NOWN:
                ffn_front(oi + 1)
            for dc in range(DC):
                pY = pb[2 + (dc % 2)]
                bY = b_pb[2 + (dc % 2)]
                emit("tensor", lambda t, pY=pY, dc=dc, wdbuf=wdbuf, hbuf=hbuf: [
                    t.matmul(pY[:, :], wdbuf[:, kc * 1024 + dc * 128: kc * 1024 + dc * 128 + 128],
                             hbuf[:, kc * NT:(kc + 1) * NT],
                             start=(kc == 0), stop=(kc == FC - 1))
                    for kc in range(FC)][-1],
                    ins=[b_wdb, b_h], outs=[bY], extra=[tok_wd])
                if e == 0:
                    emit("vector", lambda v, pY=pY, dc=dc: v.tensor_copy(
                        accf[:, dc * NT:(dc + 1) * NT], pY[:, :]),
                        ins=[bY], outs=[b_acc])
                else:
                    emit("vector", lambda v, pY=pY, dc=dc: v.tensor_add(
                        accf[:, dc * NT:(dc + 1) * NT],
                        accf[:, dc * NT:(dc + 1) * NT], pY[:, :]),
                        ins=[bY, b_acc], outs=[b_acc])

        emit("vector", lambda v: [
            v.tensor_add(x1f[:, c * NT:(c + 1) * NT], x1f[:, c * NT:(c + 1) * NT],
                         accf[:, c * NT:(c + 1) * NT]) for c in range(DC)][-1],
            ins=[b_acc], outs=[b_x1f])
        record_ln_stats(x1f, b_x1f, sqfb, b_sqfb, rows2, b_rows2, rows2_bf, b_rows2bf)
        record_ln_apply(x1f, b_x1f, [(outf, b_outf)], rows2_bf, b_rows2bf)
        emit("gpsimd", lambda g, oi=oi: g.dma_start(
            o_core.ap()[:, oi * NT:(oi + 1) * NT]
            .rearrange("(c p) n -> p c n", p=128),
            outf.rearrange("p (c n) -> p c n", n=NT)),
            ins=[b_outf], outs=[], sem="gout", n=16)

    R.op("gpsimd", None, waits=[("gout", R.ctr["gout"]), ("gy", R.ctr["gy"])])

    sems_needed = sorted(R.ctr.keys())
    return nc, R, sems_needed


_CACHED = {}


def _get_program():
    if "nc" not in _CACHED:
        nc, R, sems_needed = build_program()
        import contextlib
        stack = contextlib.ExitStack()
        sems = {name: stack.enter_context(nc.semaphore(name)) for name in sems_needed}
        block = stack.enter_context(nc.Block())
        R.emit(block, sems)
        stack.close()
        _CACHED["nc"] = nc
    return _CACHED["nc"]


def kernel(**inputs):
    x = np.asarray(inputs["L0_attn_x"])
    B, Sfull, Dm = x.shape
    assert (B, Sfull, Dm) == (4, 8192, 1024)
    x_t = np.ascontiguousarray(x.transpose(0, 2, 1))

    wk = np.ascontiguousarray(inputs["L0_attn_W_K"])
    wugg = np.ascontiguousarray(inputs["L0_attn_W_ugg"])
    wout = np.ascontiguousarray(inputs["L0_attn_W_out"])
    wr = np.ascontiguousarray(inputs["L0_ffn_W_gate_router"])
    biasr = np.ascontiguousarray(inputs["L0_ffn_expert_bias"]).reshape(1, E)
    wg_all = np.ascontiguousarray(np.concatenate(
        [inputs["L0_ffn_Wg"], inputs["L0_ffn_Wg_s"][None]], axis=0)).reshape(NE * D, F)
    wu_all = np.ascontiguousarray(np.concatenate(
        [inputs["L0_ffn_Wu"], inputs["L0_ffn_Wu_s"][None]], axis=0)).reshape(NE * D, F)
    wd_all = np.ascontiguousarray(np.concatenate(
        [inputs["L0_ffn_Wd"], inputs["L0_ffn_Wd_s"][None]], axis=0)).reshape(NE * F, D)
    rms = np.ascontiguousarray(np.asarray(inputs["L0_attn_rms_scale"]).reshape(DC, 128).T)
    split = np.ascontiguousarray(np.asarray(inputs["L0_attn_split_scale"]).reshape(DC, 128).T)
    ident = np.eye(128, dtype=np.float32)
    onehot = np.zeros((E, E * 128), dtype=np.float32)
    for e in range(E):
        onehot[e, e * 128:(e + 1) * 128] = 1.0

    shared = dict(wk_d=wk, wugg_d=wugg, wout_d=wout, wr_d=wr, biasr_d=biasr,
                  wg_d=wg_all, wu_d=wu_all, wd_d=wd_all, rms_d=rms,
                  split_d=split, ident_d=ident, onehot_d=onehot)
    in_maps = []
    for c in range(8):
        b, h = c // 2, c % 2
        if h == 1:
            xcv = x_t[b]
        else:
            xcv = np.concatenate(
                [np.zeros((D, 4096), dtype=np.float32), x_t[b][:, 0:4096]], axis=1)
        m = {"xc": np.ascontiguousarray(xcv)}
        m.update(shared)
        in_maps.append(m)

    nc = _get_program()
    res = run_bass_kernel_spmd(nc, in_maps, core_ids=list(range(8)))
    out = np.empty((4, 8192, 1024), dtype=np.float32)
    for c in range(8):
        b, h = c // 2, c % 2
        out[b, h * 4096:(h + 1) * 4096, :] = res.results[c]["o_core"].T
    return out


# version 11
# speedup vs baseline: 1.0402x; 1.0402x over previous
import numpy as np
import ml_dtypes
import concourse.bass as bass
import concourse.mybir as mybir
from concourse.bass_utils import run_bass_kernel_spmd

F32 = mybir.dt.float32
BF16 = mybir.dt.bfloat16
AX = mybir.AxisListType.X
ALU = mybir.AluOpType
ACT = mybir.ActivationFunctionType

D = 1024
H = 2048
S = 8192
NT = 512
NSUB = S // NT
NOWN = 8
DC = D // 128
HC = H // 128
HQ = 4
MQ = HC // HQ
F = 768
FC = F // 128
NE = 9
E = 8
TT128 = NT // 128

ENGS = ("sync", "gpsimd", "vector", "scalar", "tensor")


class Rec:

    def __init__(self):
        self.ops = []
        self.ctr = {}
        self.dma_sems = set()

    def op(self, eng, fn, waits=(), inc=None):
        tok = None
        if inc is not None:
            sem, n = inc
            self.ctr[sem] = self.ctr.get(sem, 0) + n
            tok = (sem, self.ctr[sem])
        self.ops.append((eng, fn, [w for w in waits if w is not None], inc))
        return tok

    def emit(self, block, sems):
        seen = {e: {} for e in ENGS}

        def section(eng_name):
            def run(engobj):
                sn = seen[eng_name]
                for (eng, fn, waits, inc) in self.ops:
                    if eng != eng_name:
                        continue
                    for (sem, val) in sorted(waits):
                        if sn.get(sem, 0) < val:
                            engobj.wait_ge(sems[sem], val)
                            sn[sem] = val
                    res = fn(engobj) if fn is not None else None
                    if inc is not None:
                        sem, n = inc
                        insts = res if isinstance(res, (list, tuple)) else [res]
                        if sem in self.dma_sems:
                            assert n == 16 * len(insts)
                            for i in insts:
                                i.then_inc(sems[sem], 16)
                        else:
                            insts[-1].then_inc(sems[sem], n)
            return run

        block.sync(section("sync"))
        block.gpsimd(section("gpsimd"))
        block.vector(section("vector"))
        block.scalar(section("scalar"))
        block.tensor(section("tensor"))


class Buf:

    def __init__(self, mem, key):
        self.mem = mem
        self.key = key
        self.wtok = None
        self.readers = []

    def overlaps(self, other):
        a, s0, e0, b0 = self.key
        a2, s1, e1, b1 = other.key
        if a != a2:
            return False
        return s0 * b0 < e1 * b1 and s1 * b1 < e0 * b0


class Mem:
    def __init__(self):
        self.bufs = []

    def buf(self, arena_id, start, end, elem_bytes=1):
        b = Buf(self, (arena_id, start, end, elem_bytes))
        self.bufs.append(b)
        return b

    def waits_for_write(self, out):
        w = []
        for b in self.bufs:
            if b.overlaps(out):
                w += b.readers
                if b.wtok is not None:
                    w.append(b.wtok)
        return w

    def mark_write(self, out, tok):
        for b in self.bufs:
            if b.overlaps(out):
                b.readers = []
                b.wtok = None
        out.wtok = tok
        out.readers = []


def build_program():
    nc = bass.Bass("TRN2", target_bir_lowering=False, debug=False,
                   detect_race_conditions=False)

    xc = nc.dram_tensor("xc", [D, S], F32, kind="ExternalInput")
    wk_d = nc.dram_tensor("wk_d", [D, H], F32, kind="ExternalInput")
    wugg_d = nc.dram_tensor("wugg_d", [D, 3 * H], F32, kind="ExternalInput")
    wout_d = nc.dram_tensor("wout_d", [H, D], F32, kind="ExternalInput")
    wr_d = nc.dram_tensor("wr_d", [D, E], F32, kind="ExternalInput")
    biasr_d = nc.dram_tensor("biasr_d", [1, E], F32, kind="ExternalInput")
    wg_d = nc.dram_tensor("wg_d", [NE * D, F], F32, kind="ExternalInput")
    wu_d = nc.dram_tensor("wu_d", [NE * D, F], F32, kind="ExternalInput")
    wd_d = nc.dram_tensor("wd_d", [NE * F, D], F32, kind="ExternalInput")
    rms_d = nc.dram_tensor("rms_d", [128, DC], F32, kind="ExternalInput")
    split_d = nc.dram_tensor("split_d", [128, DC], F32, kind="ExternalInput")
    ident_d = nc.dram_tensor("ident_d", [128, 128], F32, kind="ExternalInput")
    onehot_d = nc.dram_tensor("onehot_d", [E, E * 128], F32, kind="ExternalInput")

    o_core = nc.dram_tensor("o_core", [D, NOWN * NT], F32, kind="ExternalOutput")

    wk_bf = nc.dram_tensor("wk_bf", [D, H], BF16, kind="Internal")
    wugg_bf = nc.dram_tensor("wugg_bf", [D, 3 * H], BF16, kind="Internal")
    wout_bf = nc.dram_tensor("wout_bf", [H, D], BF16, kind="Internal")
    wg_bf = nc.dram_tensor("wg_bf", [NE * D, F], BF16, kind="Internal")
    wu_bf = nc.dram_tensor("wu_bf", [NE * D, F], BF16, kind="Internal")
    wd_bf = nc.dram_tensor("wd_bf", [NE * F, D], BF16, kind="Internal")
    y_dram = nc.dram_tensor("y_dram", [D, NOWN * NT], BF16, kind="Internal")

    xt = nc.alloc_sbuf_tensor("xt", [128, DC * NT], F32)
    state = nc.alloc_sbuf_tensor("state", [128, HC], F32)
    kprev = nc.alloc_sbuf_tensor("kprev", [128, HC], F32)
    scvec = nc.alloc_sbuf_tensor("scvec", [128, DC], F32)
    rms_t = nc.alloc_sbuf_tensor("rms_t", [128, DC], F32)
    rows = nc.alloc_sbuf_tensor("rows", [1, 4 * NT], F32)
    rows2 = nc.alloc_sbuf_tensor("rows2", [1, 4 * NT], F32)
    rows2_bf = nc.alloc_sbuf_tensor("rows2_bf", [1, 2 * NT], BF16)
    sqs = nc.alloc_sbuf_tensor("sqs", [128, 2 * NT], BF16)
    ones1 = nc.alloc_sbuf_tensor("ones1", [128, 1], F32)
    onesk1 = nc.alloc_sbuf_tensor("onesk1", [1, 128], F32)
    ones1b = nc.alloc_sbuf_tensor("ones1b", [128, 1], BF16)
    onesk1b = nc.alloc_sbuf_tensor("onesk1b", [1, 128], BF16)
    rows_bf = nc.alloc_sbuf_tensor("rows_bf", [1, 2 * NT], BF16)
    biasr = nc.alloc_sbuf_tensor("biasr", [1, E], F32)
    biasB = nc.alloc_sbuf_tensor("biasB", [128, E], F32)
    ident = nc.alloc_sbuf_tensor("ident", [128, 128], F32)
    onehot = nc.alloc_sbuf_tensor("onehot", [E, E * 128], F32)
    wr_f = nc.alloc_sbuf_tensor("wr_f", [128, DC * E], F32)
    wr_b = nc.alloc_sbuf_tensor("wr_b", [128, DC * E], BF16)
    cst = nc.alloc_sbuf_tensor("cst", [128, 6], F32)
    sctok = nc.alloc_sbuf_tensor("sctok", [128, TT128 * E], F32)
    wtok = nc.alloc_sbuf_tensor("wtok", [128, TT128 * E], F32)
    scr = nc.alloc_sbuf_tensor("scr", [128, 2 * E], F32)
    col = nc.alloc_sbuf_tensor("col", [128, 4], F32)
    cden = nc.alloc_sbuf_tensor("cden", [128, 2 * TT128], F32)
    wT = nc.alloc_sbuf_tensor("wT", [E, 2 * NT], F32)
    wsb = nc.alloc_sbuf_tensor("wsb", [128, NT], F32)
    t1 = nc.alloc_sbuf_tensor("t1", [128, NT], F32)
    t2 = nc.alloc_sbuf_tensor("t2", [128, NT], F32)

    for i, v in enumerate([1e-5, -0.5, 0.5, -1.0, 1e-30]):
        nc.const_aps.register(F32, v, cst[:, i:i + 1])

    AF = nc.alloc_sbuf_tensor("AF", [128, 12296], F32)
    AB = nc.alloc_sbuf_tensor("AB", [128, 53248], BF16)

    pb = [nc.alloc_psum_tensor(f"pb{i}", [128, NT], F32) for i in range(8)]

    mem = Mem()
    R = Rec()

    def af(start, n, name=None):
        return AF[:, start:start + n], mem.buf("AF", start, start + n, 4)

    def ab(start, n, name=None):
        return AB[:, start:start + n], mem.buf("AB", start, start + n, 2)

    def sbuf_whole(t, aid):
        n = int(np.prod(t.shape[1:]))
        return mem.buf(aid, 0, n, 4)

    b_xt = sbuf_whole(xt, "xt")
    b_state = sbuf_whole(state, "state")
    b_kprev = sbuf_whole(kprev, "kprev")
    b_rows = sbuf_whole(rows, "rows")
    b_sc = sbuf_whole(sctok, "sctok")
    b_sqs = [mem.buf("sqs", i * NT, (i + 1) * NT, 2) for i in range(2)]
    b_rowsbf = mem.buf("rows_bf", 0, 2 * NT, 2)
    b_rows2 = mem.buf("rows2", 0, 4 * NT, 4)
    b_rows2bf = mem.buf("rows2_bf", 0, 2 * NT, 2)
    b_wtok = sbuf_whole(wtok, "wtok")
    b_scr = sbuf_whole(scr, "scr")
    b_col = sbuf_whole(col, "col")
    b_cden = sbuf_whole(cden, "cden")
    b_wTS = [mem.buf("wT", i * NT, (i + 1) * NT, 4) for i in range(2)]
    b_wsb = sbuf_whole(wsb, "wsb")
    b_t1 = sbuf_whole(t1, "t1")
    b_t2 = sbuf_whole(t2, "t2")
    b_pb = [mem.buf(f"pb{i}", 0, NT, 4) for i in range(8)]
    b_ydram = mem.buf("y_dram", 0, D * NOWN * NT, 2)
    b_woutdram = mem.buf("wout_bf_dram", 0, H * D, 2)
    b_small = mem.buf("smallconsts", 0, 1, 4)

    KQW = NT + 1
    Kq = AF[:, 0:MQ * KQW]
    b_Kq = mem.buf("AF", 0, MQ * KQW, 4)
    sgq, b_sgq = af(2056, MQ * NT)
    uq, b_uq = af(4104, MQ * NT)
    u3q, b_u3q = af(6152, MQ * NT)
    soq, b_soq = af(8200, MQ * NT)
    sgoq, b_sgoq = af(10248, MQ * NT)
    zf, b_zf = af(0, DC * NT)
    x1f, b_x1f = af(4096, DC * NT)
    accf, b_acc = af(8192, DC * NT)
    outf, b_outf = af(8192, DC * NT)
    sqf, b_sqf = af(8192, DC * NT)
    wnegS = [af(0, 8 * 1024), af(8192, 4096)]


    xnbS = [ab(0, DC * NT), ab(49152, DC * NT)]
    og, b_og = ab(4096, HC * NT)
    wobS = [ab(12288 + i * HC * 128, HC * 128) for i in range(2)]
    wkqS = [ab(16384 + i * DC * NT, DC * NT) for i in range(2)]
    wuqS = [ab(24576 + i * DC * NT, DC * NT) for i in range(2)]
    wgiqS = [ab(32768 + i * DC * NT, DC * NT) for i in range(2)]
    wgoqS = [ab(40960 + i * DC * NT, DC * NT) for i in range(2)]
    ystage, b_ystage = ab(16384, DC * NT)
    ytile, b_ytile = ab(49152, DC * NT)
    x1bS = [ab(0, DC * NT), ab(40960, DC * NT)]
    hS = [ab(4096 + i * FC * NT, FC * NT) for i in range(2)]
    wgS = [ab(10240 + i * DC * F, DC * F) for i in range(2)]
    wuS = [ab(22528 + i * DC * F, DC * F) for i in range(2)]
    wdS = [ab(34816, FC * 1024), ab(34816, FC * 1024)]
    sqfb, b_sqfb = ab(49152, DC * NT)
    wnegb, b_wnegb = ab(0, 8 * 1024)

    def emit(eng, fn, ins=(), outs=(), extra=(), sem=None, n=1):
        if sem is None:
            sem = {"sync": "ldma", "gpsimd": "gps", "vector": "v",
                   "scalar": "s", "tensor": "t"}[eng]
        waits = []
        for b in ins:
            if b.wtok is not None:
                waits.append(b.wtok)
        for ob in outs:
            waits += mem.waits_for_write(ob)
        waits += [w for w in extra if w is not None]
        if eng in ("sync",) or (eng == "gpsimd" and n % 16 == 0 and n >= 16):
            R.dma_sems.add(sem)
        tok = R.op(eng, fn, waits, inc=(sem, n))
        for b in ins:
            b.readers.append(tok)
        for ob in outs:
            mem.mark_write(ob, tok)
        return tok

    emit("gpsimd", lambda g: [
        g.memset(ones1[:, :], 1.0),
        g.memset(onesk1[:, :], 1.0),
        g.memset(ones1b[:, :], 1.0),
        g.memset(onesk1b[:, :], 1.0),
        g.memset(state[:, :], 0.0),
        g.memset(kprev[:, :], 0.0),
        g.memset(cst[:, 0:1], 1e-5),
        g.memset(cst[:, 1:2], -0.5),
        g.memset(cst[:, 2:3], 0.5),
        g.memset(cst[:, 3:4], -1.0),
        g.memset(cst[:, 4:5], 1e-30),
    ][-1], outs=[b_state, b_kprev], sem="gps")
    tk_memset = R.ctr["gps"]

    tok_casts_m = emit("gpsimd", lambda g: [
        g.dma_start(wk_bf[:, :], wk_d[:, :]),
        g.dma_start(wugg_bf[:, :], wugg_d[:, :]),
    ], sem="gcm", n=16 * 2)
    tok_casts = emit("gpsimd", lambda g: [
        g.dma_start(wg_bf[:, :], wg_d[:, :]),
        g.dma_start(wu_bf[:, :], wu_d[:, :]),
        g.dma_start(wd_bf[:, :], wd_d[:, :]),
    ], sem="gcf", n=16 * 3)
    tok_small = emit("sync", lambda sy: [
        sy.dma_start(rms_t[:, :], rms_d[:, :]),
        sy.dma_start(scvec[:, :], split_d[:, :]),
        sy.dma_start(biasr[:, :], biasr_d[:, :]),
        sy.dma_start(ident[:, :], ident_d[:, :]),
        sy.dma_start(onehot[:, :], onehot_d[:, :]),
        sy.dma_start(wr_f.ap().rearrange("p (c e) -> p c e", e=E),
                     wr_d.ap().rearrange("(c p) e -> p c e", p=128)),
    ], outs=[b_small], sem="lsm", n=16 * 6)
    emit("vector", lambda v: [
        v.tensor_mul(scvec[:, :], scvec[:, :], rms_t[:, :]),
        v.tensor_copy(wr_b[:, :], wr_f[:, :]),
    ][-1], ins=[b_small], outs=[b_small], extra=[tok_small])
    tok_scvec = ("v", R.ctr["v"])
    tok_bias_mm = emit("tensor", lambda t: t.matmul(
        pb[0][:, 0:E], onesk1[0:1, :], biasr[0:1, :], start=True, stop=True),
        ins=[b_small], outs=[b_pb[0]], extra=[tok_small, (
            "gps", tk_memset)])
    emit("vector", lambda v: v.tensor_copy(biasB[:, :], pb[0][:, 0:E]),
         ins=[b_pb[0]], outs=[b_small])

    def record_ln_stats(src, b_src, sqbuf, b_sq, rw, b_rw, rwb, b_rwb):
        tok_sq = emit("gpsimd", lambda g: [
            g.tensor_mul(sqbuf[:, c * NT:(c + 1) * NT], src[:, c * NT:(c + 1) * NT],
                         src[:, c * NT:(c + 1) * NT])
            for c in range(DC)][-1], ins=[b_src], outs=[b_sq])
        tok_s1 = emit("tensor", lambda t: [
            t.matmul(pb[0][0:1, :], ones1[:, 0:1], src[:, c * NT:(c + 1) * NT],
                     start=(c == 0), stop=(c == DC - 1)) for c in range(DC)][-1],
            ins=[b_src, b_small], outs=[b_pb[0]])
        tok_s2 = emit("tensor", lambda t: [
            t.matmul(pb[1][0:1, :], ones1b[:, 0:1], sqbuf[:, c * NT:(c + 1) * NT],
                     start=(c == 0), stop=(c == DC - 1)) for c in range(DC)][-1],
            ins=[b_sq, b_small], outs=[b_pb[1]], extra=[tok_sq])
        emit("scalar", lambda s: [
            s.copy(rw[0:1, 0:NT], pb[0][0:1, :]),
            s.copy(rw[0:1, NT:2 * NT], pb[1][0:1, :]),
        ][-1], ins=[b_pb[0], b_pb[1]], outs=[b_rw], extra=[tok_s1, tok_s2])
        emit("vector", lambda v: v.tensor_scalar(
            rw[0:1, 2 * NT:3 * NT], rw[0:1, 0:NT], 1.0 / D, None, ALU.mult),
            ins=[b_rw], outs=[b_rw])
        emit("vector", lambda v: v.tensor_scalar(
            rw[0:1, 3 * NT:4 * NT], rw[0:1, NT:2 * NT], 1.0 / D, None, ALU.mult),
            ins=[b_rw], outs=[b_rw])
        emit("vector", lambda v: v.tensor_mul(
            rw[0:1, NT:2 * NT], rw[0:1, 2 * NT:3 * NT], rw[0:1, 2 * NT:3 * NT]),
            ins=[b_rw], outs=[b_rw])
        emit("vector", lambda v: v.tensor_sub(
            rw[0:1, 3 * NT:4 * NT], rw[0:1, 3 * NT:4 * NT], rw[0:1, NT:2 * NT]),
            ins=[b_rw], outs=[b_rw])
        emit("scalar", lambda s: s.activation(
            rw[0:1, NT:2 * NT], rw[0:1, 3 * NT:4 * NT], ACT.Ln, bias=1e-5),
            ins=[b_rw], outs=[b_rw])
        emit("scalar", lambda s: s.activation(
            rw[0:1, 3 * NT:4 * NT], rw[0:1, NT:2 * NT], ACT.Exp, scale=-0.5),
            ins=[b_rw], outs=[b_rw])
        emit("vector", lambda v: v.tensor_mul(
            rw[0:1, NT:2 * NT], rw[0:1, 2 * NT:3 * NT], rw[0:1, 3 * NT:4 * NT]),
            ins=[b_rw], outs=[b_rw])
        emit("vector", lambda v: [
            v.tensor_copy(rwb[0:1, 0:NT], rw[0:1, 3 * NT:4 * NT]),
            v.tensor_copy(rwb[0:1, NT:2 * NT], rw[0:1, NT:2 * NT]),
        ][-1], ins=[b_rw], outs=[b_rwb])

    def record_ln_apply(src, b_src, dst, b_dst, rwb, b_rwb):
        emit("tensor", lambda t: t.matmul(
            pb[0][:, :], onesk1b[0:1, :], rwb[0:1, 0:NT],
            start=True, stop=True), ins=[b_rwb, b_small], outs=[b_pb[0]])
        emit("tensor", lambda t: t.matmul(
            pb[1][:, :], onesk1b[0:1, :], rwb[0:1, NT:2 * NT],
            start=True, stop=True), ins=[b_rwb, b_small], outs=[b_pb[1]])
        emit("vector", lambda v: [
            v.tensor_mul(src[:, c * NT:(c + 1) * NT], src[:, c * NT:(c + 1) * NT],
                         pb[0][:, :]) for c in range(DC)][-1],
            ins=[b_pb[0]], outs=[b_src])
        emit("vector", lambda v: [
            v.tensor_sub(dst[:, c * NT:(c + 1) * NT], src[:, c * NT:(c + 1) * NT],
                         pb[1][:, :]) for c in range(DC)][-1],
            ins=[b_src, b_pb[1]], outs=[b_dst])

    def emit_xt_load(st_i):
        T0 = st_i * NT
        emit("sync", lambda sy, T0=T0: sy.dma_start(
            xt.ap().rearrange("p (c n) -> p c n", n=NT),
            xc.ap()[:, T0:T0 + NT].rearrange("(c p) n -> p c n", p=128)),
            outs=[b_xt], sem="ldx", n=16)

    def front_end(st_i):
        xnb, b_xnb = xnbS[st_i % 2]
        for c in range(DC):
            sl, b_sl = sqs[:, (c % 2) * NT:(c % 2 + 1) * NT], b_sqs[c % 2]
            emit("gpsimd", lambda g, sl=sl, c=c: g.tensor_mul(
                sl, xt[:, c * NT:(c + 1) * NT], xt[:, c * NT:(c + 1) * NT]),
                ins=[b_xt], outs=[b_sl])
            emit("tensor", lambda t, sl=sl, c=c: t.matmul(
                pb[0][0:1, :], ones1b[:, 0:1], sl,
                start=(c == 0), stop=(c == DC - 1)),
                ins=[b_sl, b_small], outs=[b_pb[0]])
        emit("scalar", lambda s: s.copy(rows[0:1, 0:NT], pb[0][0:1, :]),
             ins=[b_pb[0]], outs=[b_rows])
        emit("scalar", lambda s: s.activation(
            rows[0:1, NT:2 * NT], rows[0:1, 0:NT], ACT.Ln, bias=1e-30),
            ins=[b_rows], outs=[b_rows])
        emit("scalar", lambda s: s.activation(
            rows[0:1, 2 * NT:3 * NT], rows[0:1, NT:2 * NT], ACT.Exp, scale=0.5),
            ins=[b_rows], outs=[b_rows])
        emit("vector", lambda v: v.tensor_scalar(
            rows[0:1, 3 * NT:4 * NT], rows[0:1, 2 * NT:3 * NT], 1.0 / 32.0, 1e-8,
            ALU.mult, ALU.add), ins=[b_rows], outs=[b_rows])
        emit("scalar", lambda s: s.activation(
            rows[0:1, NT:2 * NT], rows[0:1, 3 * NT:4 * NT], ACT.Ln),
            ins=[b_rows], outs=[b_rows])
        emit("scalar", lambda s: s.activation(
            rows[0:1, 2 * NT:3 * NT], rows[0:1, NT:2 * NT], ACT.Exp, scale=-1.0),
            ins=[b_rows], outs=[b_rows])
        emit("vector", lambda v: v.tensor_copy(
            rows_bf[0:1, 0:NT], rows[0:1, 2 * NT:3 * NT]),
            ins=[b_rows], outs=[b_rowsbf])
        emit("tensor", lambda t: t.matmul(
            pb[1][:, :], onesk1b[0:1, :], rows_bf[0:1, 0:NT],
            start=True, stop=True), ins=[b_rowsbf, b_small], outs=[b_pb[1]])
        emit("vector", lambda v, xnb=xnb: [
            v.scalar_tensor_tensor(xnb[:, c * NT:(c + 1) * NT],
                                   xt[:, c * NT:(c + 1) * NT],
                                   scvec[:, c:c + 1], pb[1][:, :],
                                   ALU.mult, ALU.mult) for c in range(DC)][-1],
            ins=[b_xt, b_pb[1], b_small], outs=[b_xnb], extra=[tok_scvec])

    def quarters(st_i, qrange, xnb, b_xnb, owned):
        for q in qrange:
            qs = q * NT
            par = (st_i * HQ + q) % 2
            (wkq, b_wkq) = wkqS[par]
            (wuq, b_wuq) = wuqS[par]
            (wgiq, b_wgiq) = wgiqS[par]
            (wgoq, b_wgoq) = wgoqS[par]
            if st_i == 0:
                specs = [(wkq, b_wkq, wk_d, qs, "gqk"), (wuq, b_wuq, wugg_d, qs, "gqu"),
                         (wgiq, b_wgiq, wugg_d, H + qs, "gqg")]
            else:
                specs = [(wkq, b_wkq, wk_bf, qs, "lmk"), (wuq, b_wuq, wugg_bf, qs, "lmu"),
                         (wgiq, b_wgiq, wugg_bf, H + qs, "lmg")]
                if owned:
                    specs.append((wgoq, b_wgoq, wugg_bf, 2 * H + qs, "lmo"))
            toks = []
            for (dst, bdst, srcd, c0, sm) in specs:
                eng = "gpsimd" if st_i == 0 else "sync"
                extra = [] if st_i == 0 else [tok_casts_m]
                toks.append(emit(
                    eng,
                    lambda sy, dst=dst, srcd=srcd, c0=c0: sy.dma_start(
                        dst.rearrange("p (c m) -> p c m", m=NT),
                        srcd.ap()[:, c0:c0 + NT]
                        .rearrange("(c p) m -> p c m", p=128)),
                    outs=[bdst], sem=sm + str(par), n=16, extra=extra))
            for m in range(MQ):
                mc = q * MQ + m
                ia = [4, 5, 2, 3][m % 4]
                pA = pb[ia]
                bA = b_pb[ia]
                pB = pb[6 + (m % 2)]
                bB = b_pb[6 + (m % 2)]
                emit("tensor", lambda t, pA=pA, m=m, wkq=wkq, xnb=xnb: [
                    t.matmul(pA[:, :], wkq[:, kc * NT + m * 128: kc * NT + m * 128 + 128],
                             xnb[:, kc * NT:(kc + 1) * NT],
                             start=(kc == 0), stop=(kc == DC - 1))
                    for kc in range(DC)][-1],
                    ins=[b_wkq, b_xnb], outs=[bA], extra=[toks[0]])
                emit("scalar", lambda s, pA=pA, m=m: s.activation(
                    Kq[:, m * KQW + 1: m * KQW + 1 + NT], pA[:, :], ACT.Sigmoid),
                    ins=[bA], outs=[b_Kq])
                emit("vector", lambda v, m=m, mc=mc: v.tensor_copy(
                    Kq[:, m * KQW: m * KQW + 1], kprev[:, mc:mc + 1]),
                    ins=[b_kprev], outs=[b_Kq])
                emit("tensor", lambda t, pB=pB, m=m, wgiq=wgiq, xnb=xnb: [
                    t.matmul(pB[:, :], wgiq[:, kc * NT + m * 128: kc * NT + m * 128 + 128],
                             xnb[:, kc * NT:(kc + 1) * NT],
                             start=(kc == 0), stop=(kc == DC - 1))
                    for kc in range(DC)][-1],
                    ins=[b_wgiq, b_xnb], outs=[bB], extra=[toks[2]])
                emit("scalar", lambda s, pB=pB, m=m: s.activation(
                    sgq[:, m * NT:(m + 1) * NT], pB[:, :], ACT.Sigmoid),
                    ins=[bB], outs=[b_sgq])
                emit("tensor", lambda t, pA=pA, m=m, wuq=wuq, xnb=xnb: [
                    t.matmul(pA[:, :], wuq[:, kc * NT + m * 128: kc * NT + m * 128 + 128],
                             xnb[:, kc * NT:(kc + 1) * NT],
                             start=(kc == 0), stop=(kc == DC - 1))
                    for kc in range(DC)][-1],
                    ins=[b_wuq, b_xnb], outs=[bA], extra=[toks[1]])
                emit("vector", lambda v, pA=pA, m=m: v.tensor_mul(
                    uq[:, m * NT:(m + 1) * NT], pA[:, :], sgq[:, m * NT:(m + 1) * NT]),
                    ins=[bA, b_sgq], outs=[b_uq])
                emit("vector", lambda v, m=m: v.scalar_tensor_tensor(
                    u3q[:, m * NT:(m + 1) * NT],
                    Kq[:, m * KQW + 1: m * KQW + 1 + NT], 1.0,
                    uq[:, m * NT:(m + 1) * NT], ALU.subtract, ALU.mult),
                    ins=[b_Kq, b_uq], outs=[b_u3q])
                emit("vector", lambda v, m=m, mc=mc: v.tensor_tensor_scan(
                    soq[:, m * NT:(m + 1) * NT], Kq[:, m * KQW: m * KQW + NT],
                    u3q[:, m * NT:(m + 1) * NT], state[:, mc:mc + 1],
                    ALU.mult, ALU.add),
                    ins=[b_Kq, b_u3q, b_state], outs=[b_soq])
                emit("vector", lambda v, m=m, mc=mc: v.tensor_copy(
                    state[:, mc:mc + 1], soq[:, m * NT + NT - 1: m * NT + NT]),
                    ins=[b_soq], outs=[b_state])
                emit("vector", lambda v, m=m, mc=mc: v.tensor_copy(
                    kprev[:, mc:mc + 1], Kq[:, m * KQW + NT: m * KQW + NT + 1]),
                    ins=[b_Kq], outs=[b_kprev])
                if owned:
                    emit("tensor", lambda t, pB=pB, m=m, wgoq=wgoq, xnb=xnb: [
                        t.matmul(pB[:, :],
                                 wgoq[:, kc * NT + m * 128: kc * NT + m * 128 + 128],
                                 xnb[:, kc * NT:(kc + 1) * NT],
                                 start=(kc == 0), stop=(kc == DC - 1))
                        for kc in range(DC)][-1],
                        ins=[b_wgoq, b_xnb], outs=[bB], extra=[toks[3]])
                    emit("scalar", lambda s, pB=pB, m=m: s.activation(
                        sgoq[:, m * NT:(m + 1) * NT], pB[:, :], ACT.Sigmoid),
                        ins=[bB], outs=[b_sgoq])
                    emit("vector", lambda v, m=m, mc=mc: v.tensor_mul(
                        og[:, mc * NT:(mc + 1) * NT], soq[:, m * NT:(m + 1) * NT],
                        sgoq[:, m * NT:(m + 1) * NT]),
                        ins=[b_soq, b_sgoq], outs=[b_og])

    emit_xt_load(0)
    negT = []
    for gg in range(8):
        wneg, b_wneg = af((gg % 4) * 2048, 2 * 1024)
        wnb, b_wnb = ab(40960 + (gg % 4) * 2048, 2 * 1024)
        emit("sync", lambda sy, gg=gg, wneg=wneg: sy.dma_start(
            wneg.rearrange("p (k dd) -> p k dd", dd=1024),
            wout_d.ap()[gg * 256:(gg + 1) * 256, :]
            .rearrange("(k p) dd -> p k dd", p=128)),
            outs=[b_wneg], sem="lwn" + str(gg % 4), n=16)
        emit("vector", lambda v, wneg=wneg, wnb=wnb: [
            v.tensor_scalar(wnb[:, i * 1024:(i + 1) * 1024],
                            wneg[:, i * 1024:(i + 1) * 1024], -1.0, None, ALU.mult)
            for i in range(2)][-1], ins=[b_wneg], outs=[b_wnb])
        emit("gpsimd", lambda g, gg=gg, wnb=wnb: g.dma_start(
            wout_bf.ap()[gg * 256:(gg + 1) * 256, :]
            .rearrange("(k p) dd -> p k dd", p=128),
            wnb.rearrange("p (k dd) -> p k dd", dd=1024)),
            ins=[b_wnb], outs=[b_woutdram], sem="gwn" + str(gg % 4), n=16)
    for i in range(4):
        negT.append(("gwn" + str(i), R.ctr["gwn" + str(i)]))
    tok_woutneg = negT
    front_end(0)
    for st_i in range(NSUB):
        owned = st_i >= NSUB - NOWN
        oi = st_i - (NSUB - NOWN)
        xnb, b_xnb = xnbS[st_i % 2]
        quarters(st_i, [0, 1], xnb, b_xnb, owned)
        if st_i + 1 < NSUB:
            emit_xt_load(st_i + 1)
            front_end(st_i + 1)
        quarters(st_i, [2, 3], xnb, b_xnb, owned)
        if owned:
            for dc in range(DC):
                pY = pb[2 + (dc % 2)]
                bY = b_pb[2 + (dc % 2)]
                (wob, b_wob) = wobS[dc % 2]
                tok_wob = emit("sync", lambda sy, dc=dc, wob=wob: sy.dma_start(
                    wob.rearrange("p (k dd) -> p k dd", dd=128),
                    wout_bf.ap()[:, dc * 128:(dc + 1) * 128]
                    .rearrange("(k p) dd -> p k dd", p=128)),
                    outs=[b_wob], sem="lwo" + str(dc % 2), n=16, extra=tok_woutneg)
                emit("tensor", lambda t, pY=pY, wob=wob: [
                    t.matmul(pY[:, :], wob[:, kc * 128:(kc + 1) * 128],
                             og[:, kc * NT:(kc + 1) * NT],
                             start=(kc == 0), stop=(kc == HC - 1))
                    for kc in range(HC)][-1],
                    ins=[b_wob, b_og], outs=[bY], extra=[tok_wob])
                emit("scalar", lambda s, pY=pY, dc=dc: s.copy(
                    ystage[:, dc * NT:(dc + 1) * NT], pY[:, :]),
                    ins=[bY], outs=[b_ystage])
            emit("gpsimd", lambda g, oi=oi: g.dma_start(
                y_dram.ap()[:, oi * NT:(oi + 1) * NT]
                .rearrange("(c p) n -> p c n", p=128),
                ystage.rearrange("p (c n) -> p c n", n=NT)),
                ins=[b_ystage], outs=[b_ydram], sem="gy", n=16)

    def ffn_loads(oi):
        T0 = (NSUB - NOWN + oi) * NT
        emit("sync", lambda sy, T0=T0: sy.dma_start(
            xt.ap().rearrange("p (c n) -> p c n", n=NT),
            xc.ap()[:, T0:T0 + NT].rearrange("(c p) n -> p c n", p=128)),
            outs=[b_xt], sem="ldx", n=16)
        emit("sync", lambda sy, oi=oi: sy.dma_start(
            ytile.rearrange("p (c n) -> p c n", n=NT),
            y_dram.ap()[:, oi * NT:(oi + 1) * NT]
            .rearrange("(c p) n -> p c n", p=128)),
            ins=[b_ydram], outs=[b_ytile], sem="lyt", n=16)

    def ffn_front(oi):
        emit("gpsimd", lambda g: [
            g.tensor_scalar(zf[:, c * NT:(c + 1) * NT],
                            xt[:, c * NT:(c + 1) * NT], 2.0, None, ALU.mult)
            for c in range(DC)][-1], ins=[b_xt], outs=[b_zf])
        emit("gpsimd", lambda g: [
            g.tensor_add(zf[:, c * NT:(c + 1) * NT], zf[:, c * NT:(c + 1) * NT],
                         ytile[:, c * NT:(c + 1) * NT])
            for c in range(DC)][-1], ins=[b_ytile], outs=[b_zf])
        record_ln_stats(zf, b_zf, sqfb, b_sqfb, rows, b_rows, rows_bf, b_rowsbf)

    def ffn_front2(oi):
        x1b, b_x1b = x1bS[oi % 2]
        wTv = wT[:, (oi % 2) * NT:(oi % 2 + 1) * NT]
        b_wT = b_wTS[oi % 2]
        record_ln_apply(zf, b_zf, x1b, b_x1b, rows_bf, b_rowsbf)
        emit("tensor", lambda t, x1b=x1b: [
            t.matmul(pb[0][:, tt * E:(tt + 1) * E],
                     x1b[:, c * NT + tt * 128: c * NT + tt * 128 + 128],
                     wr_b[:, c * E:(c + 1) * E],
                     start=(c == 0), stop=(c == DC - 1))
            for tt in range(TT128) for c in range(DC)][-1],
            ins=[b_x1b, b_small], outs=[b_pb[0]])
        emit("scalar", lambda s: s.activation(
            sctok[:, :], pb[0][:, 0:TT128 * E], ACT.Sigmoid),
            ins=[b_pb[0]], outs=[b_sc])
        for tt in range(TT128):
            stile = sctok[:, tt * E:(tt + 1) * E]
            wtile = wtok[:, tt * E:(tt + 1) * E]
            emit("vector", lambda v, stile=stile, wtile=wtile: v.tensor_add(
                wtile, stile, biasB[:, :]), ins=[b_sc, b_small], outs=[b_wtok])
            emit("vector", lambda v, wtile=wtile: v.reduce_max(
                col[:, 0:1], wtile, axis=AX), ins=[b_wtok], outs=[b_col])
            emit("vector", lambda v, wtile=wtile: v.tensor_scalar(
                scr[:, 0:E], wtile, col[:, 0:1], None, ALU.is_equal),
                ins=[b_wtok, b_col], outs=[b_scr])
            emit("vector", lambda v, wtile=wtile: v.scalar_tensor_tensor(
                scr[:, E:2 * E], scr[:, 0:E], -1e30, wtile, ALU.mult, ALU.add),
                ins=[b_scr, b_wtok], outs=[b_scr])
            emit("vector", lambda v: v.reduce_max(
                col[:, 1:2], scr[:, E:2 * E], axis=AX), ins=[b_scr], outs=[b_col])
            emit("vector", lambda v, wtile=wtile: v.tensor_scalar(
                scr[:, 0:E], wtile, col[:, 1:2], None, ALU.is_ge),
                ins=[b_wtok, b_col], outs=[b_scr])
            emit("vector", lambda v, stile=stile, wtile=wtile: v.tensor_mul(
                wtile, scr[:, 0:E], stile), ins=[b_scr, b_sc], outs=[b_wtok])
            emit("vector", lambda v, tt=tt, wtile=wtile: v.reduce_sum(
                cden[:, tt:tt + 1], wtile, axis=AX), ins=[b_wtok], outs=[b_cden])
        emit("vector", lambda v: v.tensor_scalar(
            cden[:, 0:TT128], cden[:, 0:TT128], 1e-20, None, ALU.add),
            ins=[b_cden], outs=[b_cden])
        emit("scalar", lambda s: s.activation(
            cden[:, TT128:2 * TT128], cden[:, 0:TT128], ACT.Ln),
            ins=[b_cden], outs=[b_cden])
        emit("scalar", lambda s: s.activation(
            cden[:, TT128:2 * TT128], cden[:, TT128:2 * TT128], ACT.Exp, scale=-1.0),
            ins=[b_cden], outs=[b_cden])
        emit("vector", lambda v: [
            v.tensor_scalar(wtok[:, tt * E:(tt + 1) * E], wtok[:, tt * E:(tt + 1) * E],
                            cden[:, TT128 + tt:TT128 + tt + 1], None, ALU.mult)
            for tt in range(TT128)][-1], ins=[b_cden], outs=[b_wtok])
        emit("tensor", lambda t: [
            t.transpose(pb[1][0:E, tt * 128:(tt + 1) * 128],
                        wtok[:, tt * E:(tt + 1) * E], ident[:, :])
            for tt in range(TT128)][-1],
            ins=[b_wtok, b_small], outs=[b_pb[1]])
        emit("vector", lambda v, wTv=wTv: v.tensor_copy(wTv, pb[1][0:E, 0:NT]),
             ins=[b_pb[1]], outs=[b_wT])

    ffn_loads(0)
    ffn_front(0)
    ffn_front2(0)
    for oi in range(NOWN):
        x1b, b_x1b = x1bS[oi % 2]
        wTv = wT[:, (oi % 2) * NT:(oi % 2 + 1) * NT]
        b_wT = b_wTS[oi % 2]
        for e in range(NE):
            par = e % 2
            (wgbuf, b_wgb) = wgS[par]
            (wubuf, b_wub) = wuS[par]
            (wdbuf, b_wdb) = wdS[par]
            (hbuf, b_h) = hS[par]
            tok_wg = emit("sync", lambda sy, e=e, wgbuf=wgbuf: sy.dma_start(
                wgbuf.rearrange("p (c f) -> p c f", f=F),
                wg_bf.ap()[e * D:(e + 1) * D, :].rearrange("(c p) f -> p c f", p=128)),
                outs=[b_wgb], sem="lfg" + str(par), n=16, extra=[tok_casts])
            tok_wu = emit("sync", lambda sy, e=e, wubuf=wubuf: sy.dma_start(
                wubuf.rearrange("p (c f) -> p c f", f=F),
                wu_bf.ap()[e * D:(e + 1) * D, :].rearrange("(c p) f -> p c f", p=128)),
                outs=[b_wub], sem="lfu" + str(par), n=16, extra=[tok_casts])
            tok_wd = emit("sync", lambda sy, e=e, wdbuf=wdbuf: sy.dma_start(
                wdbuf.rearrange("p (k dd) -> p k dd", dd=1024),
                wd_bf.ap()[e * F:(e + 1) * F, :].rearrange("(k p) dd -> p k dd", p=128)),
                outs=[b_wdb], sem="lfd" + str(par), n=16, extra=[tok_casts])
            if e < E:
                emit("tensor", lambda t, e=e, wTv=wTv: t.matmul(
                    pb[2][:, :], onehot[:, e * 128:(e + 1) * 128], wTv,
                    start=True, stop=True), ins=[b_wT, b_small], outs=[b_pb[2]])
                emit("scalar", lambda s: s.copy(wsb[:, :], pb[2][:, :]),
                     ins=[b_pb[2]], outs=[b_wsb])
            for fc in range(FC):
                pA = pb[4 + (fc % 2)]
                pB = pb[6 + (fc % 2)]
                bA = b_pb[4 + (fc % 2)]
                bB = b_pb[6 + (fc % 2)]
                emit("tensor", lambda t, pA=pA, fc=fc, wgbuf=wgbuf, x1b=x1b: [
                    t.matmul(pA[:, :], wgbuf[:, kc * F + fc * 128: kc * F + fc * 128 + 128],
                             x1b[:, kc * NT:(kc + 1) * NT],
                             start=(kc == 0), stop=(kc == DC - 1))
                    for kc in range(DC)][-1],
                    ins=[b_wgb, b_x1b], outs=[bA], extra=[tok_wg])
                emit("tensor", lambda t, pB=pB, fc=fc, wubuf=wubuf, x1b=x1b: [
                    t.matmul(pB[:, :], wubuf[:, kc * F + fc * 128: kc * F + fc * 128 + 128],
                             x1b[:, kc * NT:(kc + 1) * NT],
                             start=(kc == 0), stop=(kc == DC - 1))
                    for kc in range(DC)][-1],
                    ins=[b_wub, b_x1b], outs=[bB], extra=[tok_wu])
                emit("scalar", lambda s, pA=pA: s.activation(
                    t1[:, :], pA[:, :], ACT.Silu), ins=[bA], outs=[b_t1])
                hslot = hbuf[:, fc * NT:(fc + 1) * NT]
                if e < E:
                    emit("vector", lambda v, pB=pB: v.tensor_mul(
                        t2[:, :], t1[:, :], pB[:, :]), ins=[b_t1, bB], outs=[b_t2])
                    emit("vector", lambda v, hslot=hslot: v.tensor_mul(
                        hslot, t2[:, :], wsb[:, :]),
                        ins=[b_t2, b_wsb], outs=[b_h])
                else:
                    emit("vector", lambda v, pB=pB, hslot=hslot: v.tensor_mul(
                        hslot, t1[:, :], pB[:, :]), ins=[b_t1, bB], outs=[b_h])
            if e == 0 and oi + 1 < NOWN:
                ffn_loads(oi + 1)
            if e == 3 and oi + 1 < NOWN:
                ffn_front(oi + 1)
            if e == 5 and oi + 1 < NOWN:
                ffn_front2(oi + 1)
            for dc in range(DC):
                pY = pb[2 + (dc % 2)]
                bY = b_pb[2 + (dc % 2)]
                emit("tensor", lambda t, pY=pY, dc=dc, wdbuf=wdbuf, hbuf=hbuf: [
                    t.matmul(pY[:, :], wdbuf[:, kc * 1024 + dc * 128: kc * 1024 + dc * 128 + 128],
                             hbuf[:, kc * NT:(kc + 1) * NT],
                             start=(kc == 0), stop=(kc == FC - 1))
                    for kc in range(FC)][-1],
                    ins=[b_wdb, b_h], outs=[bY], extra=[tok_wd])
                if e == 0:
                    emit("vector", lambda v, pY=pY, dc=dc: v.tensor_copy(
                        accf[:, dc * NT:(dc + 1) * NT], pY[:, :]),
                        ins=[bY], outs=[b_acc])
                else:
                    emit("vector", lambda v, pY=pY, dc=dc: v.tensor_add(
                        accf[:, dc * NT:(dc + 1) * NT],
                        accf[:, dc * NT:(dc + 1) * NT], pY[:, :]),
                        ins=[bY, b_acc], outs=[b_acc])

        emit("vector", lambda v, x1b=x1b: [
            v.tensor_add(accf[:, c * NT:(c + 1) * NT], accf[:, c * NT:(c + 1) * NT],
                         x1b[:, c * NT:(c + 1) * NT]) for c in range(DC)][-1],
            ins=[b_x1b], outs=[b_acc])
        record_ln_stats(accf, b_acc, sqfb, b_sqfb, rows2, b_rows2, rows2_bf, b_rows2bf)
        record_ln_apply(accf, b_acc, outf, b_outf, rows2_bf, b_rows2bf)
        emit("gpsimd", lambda g, oi=oi: g.dma_start(
            o_core.ap()[:, oi * NT:(oi + 1) * NT]
            .rearrange("(c p) n -> p c n", p=128),
            outf.rearrange("p (c n) -> p c n", n=NT)),
            ins=[b_outf], outs=[], sem="gout", n=16)

    R.op("gpsimd", None, waits=[("gout", R.ctr["gout"]), ("gy", R.ctr["gy"])])

    sems_needed = sorted(R.ctr.keys())
    return nc, R, sems_needed


_CACHED = {}


def _get_program():
    if "nc" not in _CACHED:
        nc, R, sems_needed = build_program()
        import contextlib
        stack = contextlib.ExitStack()
        sems = {name: stack.enter_context(nc.semaphore(name)) for name in sems_needed}
        block = stack.enter_context(nc.Block())
        R.emit(block, sems)
        stack.close()
        _CACHED["nc"] = nc
    return _CACHED["nc"]


def kernel(**inputs):
    x = np.asarray(inputs["L0_attn_x"])
    B, Sfull, Dm = x.shape
    assert (B, Sfull, Dm) == (4, 8192, 1024)
    x_t = np.ascontiguousarray(x.transpose(0, 2, 1))

    wk = np.ascontiguousarray(inputs["L0_attn_W_K"])
    wugg = np.ascontiguousarray(inputs["L0_attn_W_ugg"])
    wout = np.ascontiguousarray(inputs["L0_attn_W_out"])
    wr = np.ascontiguousarray(inputs["L0_ffn_W_gate_router"])
    biasr = np.ascontiguousarray(inputs["L0_ffn_expert_bias"]).reshape(1, E)
    wg_all = np.ascontiguousarray(np.concatenate(
        [inputs["L0_ffn_Wg"], inputs["L0_ffn_Wg_s"][None]], axis=0)).reshape(NE * D, F)
    wu_all = np.ascontiguousarray(np.concatenate(
        [inputs["L0_ffn_Wu"], inputs["L0_ffn_Wu_s"][None]], axis=0)).reshape(NE * D, F)
    wd_all = np.ascontiguousarray(np.concatenate(
        [inputs["L0_ffn_Wd"], inputs["L0_ffn_Wd_s"][None]], axis=0)).reshape(NE * F, D)
    rms = np.ascontiguousarray(np.asarray(inputs["L0_attn_rms_scale"]).reshape(DC, 128).T)
    split = np.ascontiguousarray(np.asarray(inputs["L0_attn_split_scale"]).reshape(DC, 128).T)
    ident = np.eye(128, dtype=np.float32)
    onehot = np.zeros((E, E * 128), dtype=np.float32)
    for e in range(E):
        onehot[e, e * 128:(e + 1) * 128] = 1.0

    shared = dict(wk_d=wk, wugg_d=wugg, wout_d=wout, wr_d=wr, biasr_d=biasr,
                  wg_d=wg_all, wu_d=wu_all, wd_d=wd_all, rms_d=rms,
                  split_d=split, ident_d=ident, onehot_d=onehot)
    in_maps = []
    for c in range(8):
        b, h = c // 2, c % 2
        if h == 1:
            xcv = x_t[b]
        else:
            xcv = np.concatenate(
                [np.zeros((D, 4096), dtype=np.float32), x_t[b][:, 0:4096]], axis=1)
        m = {"xc": np.ascontiguousarray(xcv)}
        m.update(shared)
        in_maps.append(m)

    nc = _get_program()
    res = run_bass_kernel_spmd(nc, in_maps, core_ids=list(range(8)))
    out = np.empty((4, 8192, 1024), dtype=np.float32)
    for c in range(8):
        b, h = c // 2, c % 2
        out[b, h * 4096:(h + 1) * 4096, :] = res.results[c]["o_core"].T
    return out


# version 12
# speedup vs baseline: 1.0506x; 1.0100x over previous
import numpy as np
import ml_dtypes
import concourse.bass as bass
import concourse.mybir as mybir
from concourse.bass_utils import run_bass_kernel_spmd

F32 = mybir.dt.float32
BF16 = mybir.dt.bfloat16
AX = mybir.AxisListType.X
ALU = mybir.AluOpType
ACT = mybir.ActivationFunctionType

D = 1024
H = 2048
S = 8192
NT = 512
NSUB = S // NT
NOWN = 8
DC = D // 128
HC = H // 128
HQ = 4
MQ = HC // HQ
F = 768
FC = F // 128
NE = 9
E = 8
TT128 = NT // 128

ENGS = ("sync", "gpsimd", "vector", "scalar", "tensor")


class Rec:

    def __init__(self):
        self.ops = []
        self.ctr = {}
        self.dma_sems = set()

    def op(self, eng, fn, waits=(), inc=None):
        tok = None
        if inc is not None:
            sem, n = inc
            self.ctr[sem] = self.ctr.get(sem, 0) + n
            tok = (sem, self.ctr[sem])
        self.ops.append((eng, fn, [w for w in waits if w is not None], inc))
        return tok

    def emit(self, block, sems):
        seen = {e: {} for e in ENGS}

        def section(eng_name):
            def run(engobj):
                sn = seen[eng_name]
                for (eng, fn, waits, inc) in self.ops:
                    if eng != eng_name:
                        continue
                    for (sem, val) in sorted(waits):
                        if sn.get(sem, 0) < val:
                            engobj.wait_ge(sems[sem], val)
                            sn[sem] = val
                    res = fn(engobj) if fn is not None else None
                    if inc is not None:
                        sem, n = inc
                        insts = res if isinstance(res, (list, tuple)) else [res]
                        if sem in self.dma_sems:
                            assert n == 16 * len(insts)
                            for i in insts:
                                i.then_inc(sems[sem], 16)
                        else:
                            insts[-1].then_inc(sems[sem], n)
            return run

        block.sync(section("sync"))
        block.gpsimd(section("gpsimd"))
        block.vector(section("vector"))
        block.scalar(section("scalar"))
        block.tensor(section("tensor"))


class Buf:

    def __init__(self, mem, key):
        self.mem = mem
        self.key = key
        self.wtok = None
        self.readers = []

    def overlaps(self, other):
        a, s0, e0, b0 = self.key
        a2, s1, e1, b1 = other.key
        if a != a2:
            return False
        return s0 * b0 < e1 * b1 and s1 * b1 < e0 * b0


class Mem:
    def __init__(self):
        self.bufs = []

    def buf(self, arena_id, start, end, elem_bytes=1):
        b = Buf(self, (arena_id, start, end, elem_bytes))
        self.bufs.append(b)
        return b

    def waits_for_write(self, out):
        w = []
        for b in self.bufs:
            if b.overlaps(out):
                w += b.readers
                if b.wtok is not None:
                    w.append(b.wtok)
        return w

    def mark_write(self, out, tok):
        for b in self.bufs:
            if b.overlaps(out):
                b.readers = []
                b.wtok = None
        out.wtok = tok
        out.readers = []


def build_program():
    nc = bass.Bass("TRN2", target_bir_lowering=False, debug=False,
                   detect_race_conditions=False)

    xc = nc.dram_tensor("xc", [D, S], F32, kind="ExternalInput")
    wk_d = nc.dram_tensor("wk_d", [D, H], F32, kind="ExternalInput")
    wugg_d = nc.dram_tensor("wugg_d", [D, 3 * H], F32, kind="ExternalInput")
    wout_d = nc.dram_tensor("wout_d", [H, D], F32, kind="ExternalInput")
    wr_d = nc.dram_tensor("wr_d", [D, E], F32, kind="ExternalInput")
    biasr_d = nc.dram_tensor("biasr_d", [1, E], F32, kind="ExternalInput")
    wg_d = nc.dram_tensor("wg_d", [NE * D, F], F32, kind="ExternalInput")
    wu_d = nc.dram_tensor("wu_d", [NE * D, F], F32, kind="ExternalInput")
    wd_d = nc.dram_tensor("wd_d", [NE * F, D], F32, kind="ExternalInput")
    rms_d = nc.dram_tensor("rms_d", [128, DC], F32, kind="ExternalInput")
    split_d = nc.dram_tensor("split_d", [128, DC], F32, kind="ExternalInput")
    ident_d = nc.dram_tensor("ident_d", [128, 128], F32, kind="ExternalInput")
    onehot_d = nc.dram_tensor("onehot_d", [E, E * 128], BF16, kind="ExternalInput")

    o_core = nc.dram_tensor("o_core", [D, NOWN * NT], F32, kind="ExternalOutput")

    wk_bf = nc.dram_tensor("wk_bf", [D, H], BF16, kind="Internal")
    wugg_bf = nc.dram_tensor("wugg_bf", [D, 3 * H], BF16, kind="Internal")
    wout_bf = nc.dram_tensor("wout_bf", [H, D], BF16, kind="Internal")
    wg_bf = nc.dram_tensor("wg_bf", [NE * D, F], BF16, kind="Internal")
    wu_bf = nc.dram_tensor("wu_bf", [NE * D, F], BF16, kind="Internal")
    wd_bf = nc.dram_tensor("wd_bf", [NE * F, D], BF16, kind="Internal")
    y_dram = nc.dram_tensor("y_dram", [D, NOWN * NT], BF16, kind="Internal")

    xt = nc.alloc_sbuf_tensor("xt", [128, DC * NT], F32)
    state = nc.alloc_sbuf_tensor("state", [128, HC], F32)
    kprev = nc.alloc_sbuf_tensor("kprev", [128, HC], F32)
    scvec = nc.alloc_sbuf_tensor("scvec", [128, DC], F32)
    rms_t = nc.alloc_sbuf_tensor("rms_t", [128, DC], F32)
    rows = nc.alloc_sbuf_tensor("rows", [1, 4 * NT], F32)
    rows2 = nc.alloc_sbuf_tensor("rows2", [1, 4 * NT], F32)
    rows2_bf = nc.alloc_sbuf_tensor("rows2_bf", [1, 2 * NT], BF16)
    sqs = nc.alloc_sbuf_tensor("sqs", [128, 2 * NT], BF16)
    ones1 = nc.alloc_sbuf_tensor("ones1", [128, 1], F32)
    onesk1 = nc.alloc_sbuf_tensor("onesk1", [1, 128], F32)
    ones1b = nc.alloc_sbuf_tensor("ones1b", [128, 1], BF16)
    onesk1b = nc.alloc_sbuf_tensor("onesk1b", [1, 128], BF16)
    rows_bf = nc.alloc_sbuf_tensor("rows_bf", [1, 2 * NT], BF16)
    biasr = nc.alloc_sbuf_tensor("biasr", [1, E], F32)
    biasB = nc.alloc_sbuf_tensor("biasB", [128, E], F32)
    ident = nc.alloc_sbuf_tensor("ident", [128, 128], F32)
    onehot = nc.alloc_sbuf_tensor("onehot", [E, E * 128], BF16)
    wr_f = nc.alloc_sbuf_tensor("wr_f", [128, DC * E], F32)
    wr_b = nc.alloc_sbuf_tensor("wr_b", [128, DC * E], BF16)
    cst = nc.alloc_sbuf_tensor("cst", [128, 6], F32)
    sctok = nc.alloc_sbuf_tensor("sctok", [128, TT128 * E], F32)
    wtok = nc.alloc_sbuf_tensor("wtok", [128, TT128 * E], F32)
    scr = nc.alloc_sbuf_tensor("scr", [128, 2 * E], F32)
    col = nc.alloc_sbuf_tensor("col", [128, 4], F32)
    cden = nc.alloc_sbuf_tensor("cden", [128, 2 * TT128], F32)
    wT = nc.alloc_sbuf_tensor("wT", [E, 2 * NT], BF16)
    wsb = nc.alloc_sbuf_tensor("wsb", [128, NT], F32)
    t1 = nc.alloc_sbuf_tensor("t1", [128, NT], F32)
    t2 = nc.alloc_sbuf_tensor("t2", [128, NT], F32)

    for i, v in enumerate([1e-5, -0.5, 0.5, -1.0, 1e-30]):
        nc.const_aps.register(F32, v, cst[:, i:i + 1])

    AF = nc.alloc_sbuf_tensor("AF", [128, 12296], F32)
    AB = nc.alloc_sbuf_tensor("AB", [128, 55296], BF16)

    pb = [nc.alloc_psum_tensor(f"pb{i}", [128, NT], F32) for i in range(8)]

    mem = Mem()
    R = Rec()

    def af(start, n, name=None):
        return AF[:, start:start + n], mem.buf("AF", start, start + n, 4)

    def ab(start, n, name=None):
        return AB[:, start:start + n], mem.buf("AB", start, start + n, 2)

    def sbuf_whole(t, aid):
        n = int(np.prod(t.shape[1:]))
        return mem.buf(aid, 0, n, 4)

    b_xt = sbuf_whole(xt, "xt")
    b_state = sbuf_whole(state, "state")
    b_kprev = sbuf_whole(kprev, "kprev")
    b_rows = sbuf_whole(rows, "rows")
    b_sc = sbuf_whole(sctok, "sctok")
    b_sqs = [mem.buf("sqs", i * NT, (i + 1) * NT, 2) for i in range(2)]
    b_rowsbf = mem.buf("rows_bf", 0, 2 * NT, 2)
    b_rows2 = mem.buf("rows2", 0, 4 * NT, 4)
    b_rows2bf = mem.buf("rows2_bf", 0, 2 * NT, 2)
    b_wtok = sbuf_whole(wtok, "wtok")
    b_scr = sbuf_whole(scr, "scr")
    b_col = sbuf_whole(col, "col")
    b_cden = sbuf_whole(cden, "cden")
    b_wTS = [mem.buf("wT", i * NT, (i + 1) * NT, 2) for i in range(2)]
    b_wsb = sbuf_whole(wsb, "wsb")
    b_t1 = sbuf_whole(t1, "t1")
    b_t2 = sbuf_whole(t2, "t2")
    b_pb = [mem.buf(f"pb{i}", 0, NT, 4) for i in range(8)]
    b_ydram = mem.buf("y_dram", 0, D * NOWN * NT, 2)
    b_woutdram = mem.buf("wout_bf_dram", 0, H * D, 2)
    b_small = mem.buf("smallconsts", 0, 1, 4)

    KQW = NT + 1
    Kq = AF[:, 0:MQ * KQW]
    b_Kq = mem.buf("AF", 0, MQ * KQW, 4)
    sgq, b_sgq = af(2056, MQ * NT)
    uq, b_uq = af(4104, MQ * NT)
    u3q, b_u3q = af(6152, MQ * NT)
    soq, b_soq = af(8200, MQ * NT)
    sgoq, b_sgoq = af(10248, MQ * NT)
    zf, b_zf = af(0, DC * NT)
    x1f, b_x1f = af(4096, DC * NT)
    accf, b_acc = af(8192, DC * NT)
    outf, b_outf = af(8192, DC * NT)
    sqf, b_sqf = af(8192, DC * NT)
    wnegS = [af(0, 8 * 1024), af(8192, 4096)]


    xnbS = [ab(0, DC * NT), ab(49152, DC * NT)]
    og, b_og = ab(4096, HC * NT)
    wobS = [ab(12288 + i * HC * 128, HC * 128) for i in range(2)]
    wkqS = [ab(16384 + i * DC * NT, DC * NT) for i in range(2)]
    wuqS = [ab(24576 + i * DC * NT, DC * NT) for i in range(2)]
    wgiqS = [ab(32768 + i * DC * NT, DC * NT) for i in range(2)]
    wgoqS = [ab(40960 + i * DC * NT, DC * NT) for i in range(2)]
    ystage, b_ystage = ab(16384, DC * NT)
    ytile, b_ytile = ab(51200, DC * NT)
    x1bS = [ab(0, DC * NT), ab(47104, DC * NT)]
    hS = [ab(4096 + i * FC * NT, FC * NT) for i in range(2)]
    wgS = [ab(10240 + i * DC * F, DC * F) for i in range(2)]
    wuS = [ab(22528 + i * DC * F, DC * F) for i in range(2)]
    wdS = [ab(34816 + i * FC * 1024, FC * 1024) for i in range(2)]
    sqfb, b_sqfb = ab(51200, DC * NT)
    wnegb, b_wnegb = ab(0, 8 * 1024)

    def emit(eng, fn, ins=(), outs=(), extra=(), sem=None, n=1):
        if sem is None:
            sem = {"sync": "ldma", "gpsimd": "gps", "vector": "v",
                   "scalar": "s", "tensor": "t"}[eng]
        waits = []
        for b in ins:
            if b.wtok is not None:
                waits.append(b.wtok)
        for ob in outs:
            waits += mem.waits_for_write(ob)
        waits += [w for w in extra if w is not None]
        if eng in ("sync",) or (eng == "gpsimd" and n % 16 == 0 and n >= 16):
            R.dma_sems.add(sem)
        tok = R.op(eng, fn, waits, inc=(sem, n))
        for b in ins:
            b.readers.append(tok)
        for ob in outs:
            mem.mark_write(ob, tok)
        return tok

    emit("gpsimd", lambda g: [
        g.memset(ones1[:, :], 1.0),
        g.memset(onesk1[:, :], 1.0),
        g.memset(ones1b[:, :], 1.0),
        g.memset(onesk1b[:, :], 1.0),
        g.memset(state[:, :], 0.0),
        g.memset(kprev[:, :], 0.0),
        g.memset(cst[:, 0:1], 1e-5),
        g.memset(cst[:, 1:2], -0.5),
        g.memset(cst[:, 2:3], 0.5),
        g.memset(cst[:, 3:4], -1.0),
        g.memset(cst[:, 4:5], 1e-30),
    ][-1], outs=[b_state, b_kprev], sem="gps")
    tk_memset = R.ctr["gps"]

    tok_casts_m = emit("gpsimd", lambda g: [
        g.dma_start(wk_bf[:, :], wk_d[:, :]),
        g.dma_start(wugg_bf[:, :], wugg_d[:, :]),
    ], sem="gcm", n=16 * 2)
    tok_casts = emit("gpsimd", lambda g: [
        g.dma_start(wg_bf[:, :], wg_d[:, :]),
        g.dma_start(wu_bf[:, :], wu_d[:, :]),
        g.dma_start(wd_bf[:, :], wd_d[:, :]),
    ], sem="gcf", n=16 * 3)
    tok_small = emit("sync", lambda sy: [
        sy.dma_start(rms_t[:, :], rms_d[:, :]),
        sy.dma_start(scvec[:, :], split_d[:, :]),
        sy.dma_start(biasr[:, :], biasr_d[:, :]),
        sy.dma_start(ident[:, :], ident_d[:, :]),
        sy.dma_start(onehot[:, :], onehot_d[:, :]),
        sy.dma_start(wr_f.ap().rearrange("p (c e) -> p c e", e=E),
                     wr_d.ap().rearrange("(c p) e -> p c e", p=128)),
    ], outs=[b_small], sem="lsm", n=16 * 6)
    emit("vector", lambda v: [
        v.tensor_mul(scvec[:, :], scvec[:, :], rms_t[:, :]),
        v.tensor_copy(wr_b[:, :], wr_f[:, :]),
    ][-1], ins=[b_small], outs=[b_small], extra=[tok_small])
    tok_scvec = ("v", R.ctr["v"])
    tok_bias_mm = emit("tensor", lambda t: t.matmul(
        pb[0][:, 0:E], onesk1[0:1, :], biasr[0:1, :], start=True, stop=True),
        ins=[b_small], outs=[b_pb[0]], extra=[tok_small, (
            "gps", tk_memset)])
    emit("vector", lambda v: v.tensor_copy(biasB[:, :], pb[0][:, 0:E]),
         ins=[b_pb[0]], outs=[b_small])

    def record_ln_stats(src, b_src, sqbuf, b_sq, rw, b_rw, rwb, b_rwb):
        tok_sq = emit("gpsimd", lambda g: [
            g.tensor_mul(sqbuf[:, c * NT:(c + 1) * NT], src[:, c * NT:(c + 1) * NT],
                         src[:, c * NT:(c + 1) * NT])
            for c in range(DC)][-1], ins=[b_src], outs=[b_sq])
        tok_s1 = emit("tensor", lambda t: [
            t.matmul(pb[0][0:1, :], ones1[:, 0:1], src[:, c * NT:(c + 1) * NT],
                     start=(c == 0), stop=(c == DC - 1)) for c in range(DC)][-1],
            ins=[b_src, b_small], outs=[b_pb[0]])
        tok_s2 = emit("tensor", lambda t: [
            t.matmul(pb[1][0:1, :], ones1b[:, 0:1], sqbuf[:, c * NT:(c + 1) * NT],
                     start=(c == 0), stop=(c == DC - 1)) for c in range(DC)][-1],
            ins=[b_sq, b_small], outs=[b_pb[1]], extra=[tok_sq])
        emit("scalar", lambda s: [
            s.copy(rw[0:1, 0:NT], pb[0][0:1, :]),
            s.copy(rw[0:1, NT:2 * NT], pb[1][0:1, :]),
        ][-1], ins=[b_pb[0], b_pb[1]], outs=[b_rw], extra=[tok_s1, tok_s2])
        emit("vector", lambda v: v.tensor_scalar(
            rw[0:1, 2 * NT:3 * NT], rw[0:1, 0:NT], 1.0 / D, None, ALU.mult),
            ins=[b_rw], outs=[b_rw])
        emit("vector", lambda v: v.tensor_scalar(
            rw[0:1, 3 * NT:4 * NT], rw[0:1, NT:2 * NT], 1.0 / D, None, ALU.mult),
            ins=[b_rw], outs=[b_rw])
        emit("vector", lambda v: v.tensor_mul(
            rw[0:1, NT:2 * NT], rw[0:1, 2 * NT:3 * NT], rw[0:1, 2 * NT:3 * NT]),
            ins=[b_rw], outs=[b_rw])
        emit("vector", lambda v: v.tensor_sub(
            rw[0:1, 3 * NT:4 * NT], rw[0:1, 3 * NT:4 * NT], rw[0:1, NT:2 * NT]),
            ins=[b_rw], outs=[b_rw])
        emit("scalar", lambda s: s.activation(
            rw[0:1, NT:2 * NT], rw[0:1, 3 * NT:4 * NT], ACT.Ln, bias=1e-5),
            ins=[b_rw], outs=[b_rw])
        emit("scalar", lambda s: s.activation(
            rw[0:1, 3 * NT:4 * NT], rw[0:1, NT:2 * NT], ACT.Exp, scale=-0.5),
            ins=[b_rw], outs=[b_rw])
        emit("vector", lambda v: v.tensor_mul(
            rw[0:1, NT:2 * NT], rw[0:1, 2 * NT:3 * NT], rw[0:1, 3 * NT:4 * NT]),
            ins=[b_rw], outs=[b_rw])
        emit("vector", lambda v: [
            v.tensor_copy(rwb[0:1, 0:NT], rw[0:1, 3 * NT:4 * NT]),
            v.tensor_copy(rwb[0:1, NT:2 * NT], rw[0:1, NT:2 * NT]),
        ][-1], ins=[b_rw], outs=[b_rwb])

    def record_ln_apply(src, b_src, dst, b_dst, rwb, b_rwb):
        emit("tensor", lambda t: t.matmul(
            pb[0][:, :], onesk1b[0:1, :], rwb[0:1, 0:NT],
            start=True, stop=True), ins=[b_rwb, b_small], outs=[b_pb[0]])
        emit("tensor", lambda t: t.matmul(
            pb[1][:, :], onesk1b[0:1, :], rwb[0:1, NT:2 * NT],
            start=True, stop=True), ins=[b_rwb, b_small], outs=[b_pb[1]])
        emit("vector", lambda v: [
            v.tensor_mul(src[:, c * NT:(c + 1) * NT], src[:, c * NT:(c + 1) * NT],
                         pb[0][:, :]) for c in range(DC)][-1],
            ins=[b_pb[0]], outs=[b_src])
        emit("vector", lambda v: [
            v.tensor_sub(dst[:, c * NT:(c + 1) * NT], src[:, c * NT:(c + 1) * NT],
                         pb[1][:, :]) for c in range(DC)][-1],
            ins=[b_src, b_pb[1]], outs=[b_dst])

    def emit_xt_load(st_i):
        T0 = st_i * NT
        emit("sync", lambda sy, T0=T0: sy.dma_start(
            xt.ap().rearrange("p (c n) -> p c n", n=NT),
            xc.ap()[:, T0:T0 + NT].rearrange("(c p) n -> p c n", p=128)),
            outs=[b_xt], sem="ldx", n=16)

    def front_end(st_i):
        xnb, b_xnb = xnbS[st_i % 2]
        for c in range(DC):
            sl, b_sl = sqs[:, (c % 2) * NT:(c % 2 + 1) * NT], b_sqs[c % 2]
            emit("gpsimd", lambda g, sl=sl, c=c: g.tensor_mul(
                sl, xt[:, c * NT:(c + 1) * NT], xt[:, c * NT:(c + 1) * NT]),
                ins=[b_xt], outs=[b_sl])
            emit("tensor", lambda t, sl=sl, c=c: t.matmul(
                pb[0][0:1, :], ones1b[:, 0:1], sl,
                start=(c == 0), stop=(c == DC - 1)),
                ins=[b_sl, b_small], outs=[b_pb[0]])
        emit("scalar", lambda s: s.copy(rows[0:1, 0:NT], pb[0][0:1, :]),
             ins=[b_pb[0]], outs=[b_rows])
        emit("scalar", lambda s: s.activation(
            rows[0:1, NT:2 * NT], rows[0:1, 0:NT], ACT.Ln, bias=1e-30),
            ins=[b_rows], outs=[b_rows])
        emit("scalar", lambda s: s.activation(
            rows[0:1, 2 * NT:3 * NT], rows[0:1, NT:2 * NT], ACT.Exp, scale=0.5),
            ins=[b_rows], outs=[b_rows])
        emit("vector", lambda v: v.tensor_scalar(
            rows[0:1, 3 * NT:4 * NT], rows[0:1, 2 * NT:3 * NT], 1.0 / 32.0, 1e-8,
            ALU.mult, ALU.add), ins=[b_rows], outs=[b_rows])
        emit("scalar", lambda s: s.activation(
            rows[0:1, NT:2 * NT], rows[0:1, 3 * NT:4 * NT], ACT.Ln),
            ins=[b_rows], outs=[b_rows])
        emit("scalar", lambda s: s.activation(
            rows[0:1, 2 * NT:3 * NT], rows[0:1, NT:2 * NT], ACT.Exp, scale=-1.0),
            ins=[b_rows], outs=[b_rows])
        emit("vector", lambda v: v.tensor_copy(
            rows_bf[0:1, 0:NT], rows[0:1, 2 * NT:3 * NT]),
            ins=[b_rows], outs=[b_rowsbf])
        emit("tensor", lambda t: t.matmul(
            pb[1][:, :], onesk1b[0:1, :], rows_bf[0:1, 0:NT],
            start=True, stop=True), ins=[b_rowsbf, b_small], outs=[b_pb[1]])
        emit("vector", lambda v, xnb=xnb: [
            v.scalar_tensor_tensor(xnb[:, c * NT:(c + 1) * NT],
                                   xt[:, c * NT:(c + 1) * NT],
                                   scvec[:, c:c + 1], pb[1][:, :],
                                   ALU.mult, ALU.mult) for c in range(DC)][-1],
            ins=[b_xt, b_pb[1], b_small], outs=[b_xnb], extra=[tok_scvec])

    def quarters(st_i, qrange, xnb, b_xnb, owned):
        for q in qrange:
            qs = q * NT
            par = (st_i * HQ + q) % 2
            (wkq, b_wkq) = wkqS[par]
            (wuq, b_wuq) = wuqS[par]
            (wgiq, b_wgiq) = wgiqS[par]
            (wgoq, b_wgoq) = wgoqS[par]
            if st_i == 0:
                specs = [(wkq, b_wkq, wk_d, qs, "gqk"), (wuq, b_wuq, wugg_d, qs, "gqu"),
                         (wgiq, b_wgiq, wugg_d, H + qs, "gqg")]
            else:
                specs = [(wkq, b_wkq, wk_bf, qs, "lmk"), (wuq, b_wuq, wugg_bf, qs, "lmu"),
                         (wgiq, b_wgiq, wugg_bf, H + qs, "lmg")]
                if owned:
                    specs.append((wgoq, b_wgoq, wugg_bf, 2 * H + qs, "lmo"))
            toks = []
            for (dst, bdst, srcd, c0, sm) in specs:
                eng = "gpsimd" if st_i == 0 else "sync"
                extra = [] if st_i == 0 else [tok_casts_m]
                toks.append(emit(
                    eng,
                    lambda sy, dst=dst, srcd=srcd, c0=c0: sy.dma_start(
                        dst.rearrange("p (c m) -> p c m", m=NT),
                        srcd.ap()[:, c0:c0 + NT]
                        .rearrange("(c p) m -> p c m", p=128)),
                    outs=[bdst], sem=sm + str(par), n=16, extra=extra))
            for m in range(MQ):
                mc = q * MQ + m
                ia = [4, 5, 2, 3][m % 4]
                pA = pb[ia]
                bA = b_pb[ia]
                pB = pb[6 + (m % 2)]
                bB = b_pb[6 + (m % 2)]
                emit("tensor", lambda t, pA=pA, m=m, wkq=wkq, xnb=xnb: [
                    t.matmul(pA[:, :], wkq[:, kc * NT + m * 128: kc * NT + m * 128 + 128],
                             xnb[:, kc * NT:(kc + 1) * NT],
                             start=(kc == 0), stop=(kc == DC - 1))
                    for kc in range(DC)][-1],
                    ins=[b_wkq, b_xnb], outs=[bA], extra=[toks[0]])
                emit("scalar", lambda s, pA=pA, m=m: s.activation(
                    Kq[:, m * KQW + 1: m * KQW + 1 + NT], pA[:, :], ACT.Sigmoid),
                    ins=[bA], outs=[b_Kq])
                emit("vector", lambda v, m=m, mc=mc: v.tensor_copy(
                    Kq[:, m * KQW: m * KQW + 1], kprev[:, mc:mc + 1]),
                    ins=[b_kprev], outs=[b_Kq])
                emit("tensor", lambda t, pB=pB, m=m, wgiq=wgiq, xnb=xnb: [
                    t.matmul(pB[:, :], wgiq[:, kc * NT + m * 128: kc * NT + m * 128 + 128],
                             xnb[:, kc * NT:(kc + 1) * NT],
                             start=(kc == 0), stop=(kc == DC - 1))
                    for kc in range(DC)][-1],
                    ins=[b_wgiq, b_xnb], outs=[bB], extra=[toks[2]])
                emit("scalar", lambda s, pB=pB, m=m: s.activation(
                    sgq[:, m * NT:(m + 1) * NT], pB[:, :], ACT.Sigmoid),
                    ins=[bB], outs=[b_sgq])
                emit("tensor", lambda t, pA=pA, m=m, wuq=wuq, xnb=xnb: [
                    t.matmul(pA[:, :], wuq[:, kc * NT + m * 128: kc * NT + m * 128 + 128],
                             xnb[:, kc * NT:(kc + 1) * NT],
                             start=(kc == 0), stop=(kc == DC - 1))
                    for kc in range(DC)][-1],
                    ins=[b_wuq, b_xnb], outs=[bA], extra=[toks[1]])
                emit("vector", lambda v, pA=pA, m=m: v.tensor_mul(
                    uq[:, m * NT:(m + 1) * NT], pA[:, :], sgq[:, m * NT:(m + 1) * NT]),
                    ins=[bA, b_sgq], outs=[b_uq])
                emit("vector", lambda v, m=m: v.scalar_tensor_tensor(
                    u3q[:, m * NT:(m + 1) * NT],
                    Kq[:, m * KQW + 1: m * KQW + 1 + NT], 1.0,
                    uq[:, m * NT:(m + 1) * NT], ALU.subtract, ALU.mult),
                    ins=[b_Kq, b_uq], outs=[b_u3q])
                emit("vector", lambda v, m=m, mc=mc: v.tensor_tensor_scan(
                    soq[:, m * NT:(m + 1) * NT], Kq[:, m * KQW: m * KQW + NT],
                    u3q[:, m * NT:(m + 1) * NT], state[:, mc:mc + 1],
                    ALU.mult, ALU.add),
                    ins=[b_Kq, b_u3q, b_state], outs=[b_soq])
                emit("vector", lambda v, m=m, mc=mc: v.tensor_copy(
                    state[:, mc:mc + 1], soq[:, m * NT + NT - 1: m * NT + NT]),
                    ins=[b_soq], outs=[b_state])
                emit("vector", lambda v, m=m, mc=mc: v.tensor_copy(
                    kprev[:, mc:mc + 1], Kq[:, m * KQW + NT: m * KQW + NT + 1]),
                    ins=[b_Kq], outs=[b_kprev])
                if owned:
                    emit("tensor", lambda t, pB=pB, m=m, wgoq=wgoq, xnb=xnb: [
                        t.matmul(pB[:, :],
                                 wgoq[:, kc * NT + m * 128: kc * NT + m * 128 + 128],
                                 xnb[:, kc * NT:(kc + 1) * NT],
                                 start=(kc == 0), stop=(kc == DC - 1))
                        for kc in range(DC)][-1],
                        ins=[b_wgoq, b_xnb], outs=[bB], extra=[toks[3]])
                    emit("scalar", lambda s, pB=pB, m=m: s.activation(
                        sgoq[:, m * NT:(m + 1) * NT], pB[:, :], ACT.Sigmoid),
                        ins=[bB], outs=[b_sgoq])
                    emit("vector", lambda v, m=m, mc=mc: v.tensor_mul(
                        og[:, mc * NT:(mc + 1) * NT], soq[:, m * NT:(m + 1) * NT],
                        sgoq[:, m * NT:(m + 1) * NT]),
                        ins=[b_soq, b_sgoq], outs=[b_og])

    emit_xt_load(0)
    negT = []
    for gg in range(8):
        wneg, b_wneg = af((gg % 4) * 2048, 2 * 1024)
        wnb, b_wnb = ab(40960 + (gg % 4) * 2048, 2 * 1024)
        emit("sync", lambda sy, gg=gg, wneg=wneg: sy.dma_start(
            wneg.rearrange("p (k dd) -> p k dd", dd=1024),
            wout_d.ap()[gg * 256:(gg + 1) * 256, :]
            .rearrange("(k p) dd -> p k dd", p=128)),
            outs=[b_wneg], sem="lwn" + str(gg % 4), n=16)
        emit("vector", lambda v, wneg=wneg, wnb=wnb: [
            v.tensor_scalar(wnb[:, i * 1024:(i + 1) * 1024],
                            wneg[:, i * 1024:(i + 1) * 1024], -1.0, None, ALU.mult)
            for i in range(2)][-1], ins=[b_wneg], outs=[b_wnb])
        emit("gpsimd", lambda g, gg=gg, wnb=wnb: g.dma_start(
            wout_bf.ap()[gg * 256:(gg + 1) * 256, :]
            .rearrange("(k p) dd -> p k dd", p=128),
            wnb.rearrange("p (k dd) -> p k dd", dd=1024)),
            ins=[b_wnb], outs=[b_woutdram], sem="gwn" + str(gg % 4), n=16)
    for i in range(4):
        negT.append(("gwn" + str(i), R.ctr["gwn" + str(i)]))
    tok_woutneg = negT
    front_end(0)
    for st_i in range(NSUB):
        owned = st_i >= NSUB - NOWN
        oi = st_i - (NSUB - NOWN)
        xnb, b_xnb = xnbS[st_i % 2]
        quarters(st_i, [0, 1], xnb, b_xnb, owned)
        if st_i + 1 < NSUB:
            emit_xt_load(st_i + 1)
            front_end(st_i + 1)
        quarters(st_i, [2, 3], xnb, b_xnb, owned)
        if owned:
            for dc in range(DC):
                pY = pb[2 + (dc % 2)]
                bY = b_pb[2 + (dc % 2)]
                (wob, b_wob) = wobS[dc % 2]
                tok_wob = emit("sync", lambda sy, dc=dc, wob=wob: sy.dma_start(
                    wob.rearrange("p (k dd) -> p k dd", dd=128),
                    wout_bf.ap()[:, dc * 128:(dc + 1) * 128]
                    .rearrange("(k p) dd -> p k dd", p=128)),
                    outs=[b_wob], sem="lwo" + str(dc % 2), n=16, extra=tok_woutneg)
                emit("tensor", lambda t, pY=pY, wob=wob: [
                    t.matmul(pY[:, :], wob[:, kc * 128:(kc + 1) * 128],
                             og[:, kc * NT:(kc + 1) * NT],
                             start=(kc == 0), stop=(kc == HC - 1))
                    for kc in range(HC)][-1],
                    ins=[b_wob, b_og], outs=[bY], extra=[tok_wob])
                emit("scalar", lambda s, pY=pY, dc=dc: s.copy(
                    ystage[:, dc * NT:(dc + 1) * NT], pY[:, :]),
                    ins=[bY], outs=[b_ystage])
            emit("gpsimd", lambda g, oi=oi: g.dma_start(
                y_dram.ap()[:, oi * NT:(oi + 1) * NT]
                .rearrange("(c p) n -> p c n", p=128),
                ystage.rearrange("p (c n) -> p c n", n=NT)),
                ins=[b_ystage], outs=[b_ydram], sem="gy", n=16)

    def ffn_loads(oi):
        T0 = (NSUB - NOWN + oi) * NT
        emit("sync", lambda sy, T0=T0: sy.dma_start(
            xt.ap().rearrange("p (c n) -> p c n", n=NT),
            xc.ap()[:, T0:T0 + NT].rearrange("(c p) n -> p c n", p=128)),
            outs=[b_xt], sem="ldx", n=16)
        emit("sync", lambda sy, oi=oi: sy.dma_start(
            ytile.rearrange("p (c n) -> p c n", n=NT),
            y_dram.ap()[:, oi * NT:(oi + 1) * NT]
            .rearrange("(c p) n -> p c n", p=128)),
            ins=[b_ydram], outs=[b_ytile], sem="lyt", n=16)

    def ffn_front(oi):
        emit("gpsimd", lambda g: [
            g.tensor_scalar(zf[:, c * NT:(c + 1) * NT],
                            xt[:, c * NT:(c + 1) * NT], 2.0, None, ALU.mult)
            for c in range(DC)][-1], ins=[b_xt], outs=[b_zf])
        emit("gpsimd", lambda g: [
            g.tensor_add(zf[:, c * NT:(c + 1) * NT], zf[:, c * NT:(c + 1) * NT],
                         ytile[:, c * NT:(c + 1) * NT])
            for c in range(DC)][-1], ins=[b_ytile], outs=[b_zf])
        record_ln_stats(zf, b_zf, sqfb, b_sqfb, rows, b_rows, rows_bf, b_rowsbf)

    def ffn_front2(oi):
        x1b, b_x1b = x1bS[oi % 2]
        wTv = wT[:, (oi % 2) * NT:(oi % 2 + 1) * NT]
        b_wT = b_wTS[oi % 2]
        record_ln_apply(zf, b_zf, x1b, b_x1b, rows_bf, b_rowsbf)
        emit("tensor", lambda t, x1b=x1b: [
            t.matmul(pb[0][:, tt * E:(tt + 1) * E],
                     x1b[:, c * NT + tt * 128: c * NT + tt * 128 + 128],
                     wr_b[:, c * E:(c + 1) * E],
                     start=(c == 0), stop=(c == DC - 1))
            for tt in range(TT128) for c in range(DC)][-1],
            ins=[b_x1b, b_small], outs=[b_pb[0]])
        emit("scalar", lambda s: s.activation(
            sctok[:, :], pb[0][:, 0:TT128 * E], ACT.Sigmoid),
            ins=[b_pb[0]], outs=[b_sc])
        for tt in range(TT128):
            stile = sctok[:, tt * E:(tt + 1) * E]
            wtile = wtok[:, tt * E:(tt + 1) * E]
            emit("vector", lambda v, stile=stile, wtile=wtile: v.tensor_add(
                wtile, stile, biasB[:, :]), ins=[b_sc, b_small], outs=[b_wtok])
            emit("vector", lambda v, wtile=wtile: v.reduce_max(
                col[:, 0:1], wtile, axis=AX), ins=[b_wtok], outs=[b_col])
            emit("vector", lambda v, wtile=wtile: v.tensor_scalar(
                scr[:, 0:E], wtile, col[:, 0:1], None, ALU.is_equal),
                ins=[b_wtok, b_col], outs=[b_scr])
            emit("vector", lambda v, wtile=wtile: v.scalar_tensor_tensor(
                scr[:, E:2 * E], scr[:, 0:E], -1e30, wtile, ALU.mult, ALU.add),
                ins=[b_scr, b_wtok], outs=[b_scr])
            emit("vector", lambda v: v.reduce_max(
                col[:, 1:2], scr[:, E:2 * E], axis=AX), ins=[b_scr], outs=[b_col])
            emit("vector", lambda v, wtile=wtile: v.tensor_scalar(
                scr[:, 0:E], wtile, col[:, 1:2], None, ALU.is_ge),
                ins=[b_wtok, b_col], outs=[b_scr])
            emit("vector", lambda v, stile=stile, wtile=wtile: v.tensor_mul(
                wtile, scr[:, 0:E], stile), ins=[b_scr, b_sc], outs=[b_wtok])
            emit("vector", lambda v, tt=tt, wtile=wtile: v.reduce_sum(
                cden[:, tt:tt + 1], wtile, axis=AX), ins=[b_wtok], outs=[b_cden])
        emit("vector", lambda v: v.tensor_scalar(
            cden[:, 0:TT128], cden[:, 0:TT128], 1e-20, None, ALU.add),
            ins=[b_cden], outs=[b_cden])
        emit("scalar", lambda s: s.activation(
            cden[:, TT128:2 * TT128], cden[:, 0:TT128], ACT.Ln),
            ins=[b_cden], outs=[b_cden])
        emit("scalar", lambda s: s.activation(
            cden[:, TT128:2 * TT128], cden[:, TT128:2 * TT128], ACT.Exp, scale=-1.0),
            ins=[b_cden], outs=[b_cden])
        emit("vector", lambda v: [
            v.tensor_scalar(wtok[:, tt * E:(tt + 1) * E], wtok[:, tt * E:(tt + 1) * E],
                            cden[:, TT128 + tt:TT128 + tt + 1], None, ALU.mult)
            for tt in range(TT128)][-1], ins=[b_cden], outs=[b_wtok])
        emit("tensor", lambda t: [
            t.transpose(pb[1][0:E, tt * 128:(tt + 1) * 128],
                        wtok[:, tt * E:(tt + 1) * E], ident[:, :])
            for tt in range(TT128)][-1],
            ins=[b_wtok, b_small], outs=[b_pb[1]])
        emit("vector", lambda v, wTv=wTv: v.tensor_copy(wTv, pb[1][0:E, 0:NT]),
             ins=[b_pb[1]], outs=[b_wT])

    ffn_loads(0)
    ffn_front(0)
    ffn_front2(0)
    for oi in range(NOWN):
        x1b, b_x1b = x1bS[oi % 2]
        wTv = wT[:, (oi % 2) * NT:(oi % 2 + 1) * NT]
        b_wT = b_wTS[oi % 2]
        for e in range(NE):
            par = e % 2
            (wgbuf, b_wgb) = wgS[par]
            (wubuf, b_wub) = wuS[par]
            (wdbuf, b_wdb) = wdS[par]
            (hbuf, b_h) = hS[par]
            tok_wg = emit("sync", lambda sy, e=e, wgbuf=wgbuf: sy.dma_start(
                wgbuf.rearrange("p (c f) -> p c f", f=F),
                wg_bf.ap()[e * D:(e + 1) * D, :].rearrange("(c p) f -> p c f", p=128)),
                outs=[b_wgb], sem="lfg" + str(par), n=16, extra=[tok_casts])
            tok_wu = emit("sync", lambda sy, e=e, wubuf=wubuf: sy.dma_start(
                wubuf.rearrange("p (c f) -> p c f", f=F),
                wu_bf.ap()[e * D:(e + 1) * D, :].rearrange("(c p) f -> p c f", p=128)),
                outs=[b_wub], sem="lfu" + str(par), n=16, extra=[tok_casts])
            tok_wd = emit("sync", lambda sy, e=e, wdbuf=wdbuf: sy.dma_start(
                wdbuf.rearrange("p (k dd) -> p k dd", dd=1024),
                wd_bf.ap()[e * F:(e + 1) * F, :].rearrange("(k p) dd -> p k dd", p=128)),
                outs=[b_wdb], sem="lfd" + str(par), n=16, extra=[tok_casts])
            if e < E:
                emit("tensor", lambda t, e=e, wTv=wTv: t.matmul(
                    pb[2][:, :], onehot[:, e * 128:(e + 1) * 128], wTv,
                    start=True, stop=True), ins=[b_wT, b_small], outs=[b_pb[2]])
                emit("scalar", lambda s: s.copy(wsb[:, :], pb[2][:, :]),
                     ins=[b_pb[2]], outs=[b_wsb])
            for fc in range(FC):
                pA = pb[4 + (fc % 2)]
                pB = pb[6 + (fc % 2)]
                bA = b_pb[4 + (fc % 2)]
                bB = b_pb[6 + (fc % 2)]
                emit("tensor", lambda t, pA=pA, fc=fc, wgbuf=wgbuf, x1b=x1b: [
                    t.matmul(pA[:, :], wgbuf[:, kc * F + fc * 128: kc * F + fc * 128 + 128],
                             x1b[:, kc * NT:(kc + 1) * NT],
                             start=(kc == 0), stop=(kc == DC - 1))
                    for kc in range(DC)][-1],
                    ins=[b_wgb, b_x1b], outs=[bA], extra=[tok_wg])
                emit("tensor", lambda t, pB=pB, fc=fc, wubuf=wubuf, x1b=x1b: [
                    t.matmul(pB[:, :], wubuf[:, kc * F + fc * 128: kc * F + fc * 128 + 128],
                             x1b[:, kc * NT:(kc + 1) * NT],
                             start=(kc == 0), stop=(kc == DC - 1))
                    for kc in range(DC)][-1],
                    ins=[b_wub, b_x1b], outs=[bB], extra=[tok_wu])
                emit("scalar", lambda s, pA=pA: s.activation(
                    t1[:, :], pA[:, :], ACT.Silu), ins=[bA], outs=[b_t1])
                hslot = hbuf[:, fc * NT:(fc + 1) * NT]
                if e < E:
                    emit("vector", lambda v, pB=pB: v.tensor_mul(
                        t2[:, :], t1[:, :], pB[:, :]), ins=[b_t1, bB], outs=[b_t2])
                    emit("vector", lambda v, hslot=hslot: v.tensor_mul(
                        hslot, t2[:, :], wsb[:, :]),
                        ins=[b_t2, b_wsb], outs=[b_h])
                else:
                    emit("vector", lambda v, pB=pB, hslot=hslot: v.tensor_mul(
                        hslot, t1[:, :], pB[:, :]), ins=[b_t1, bB], outs=[b_h])
            if e == 0 and oi + 1 < NOWN:
                ffn_loads(oi + 1)
            if e == 3 and oi + 1 < NOWN:
                ffn_front(oi + 1)
            if e == 5 and oi + 1 < NOWN:
                ffn_front2(oi + 1)
            for dc in range(DC):
                pY = pb[2 + (dc % 2)]
                bY = b_pb[2 + (dc % 2)]
                emit("tensor", lambda t, pY=pY, dc=dc, wdbuf=wdbuf, hbuf=hbuf: [
                    t.matmul(pY[:, :], wdbuf[:, kc * 1024 + dc * 128: kc * 1024 + dc * 128 + 128],
                             hbuf[:, kc * NT:(kc + 1) * NT],
                             start=(kc == 0), stop=(kc == FC - 1))
                    for kc in range(FC)][-1],
                    ins=[b_wdb, b_h], outs=[bY], extra=[tok_wd])
                if e == 0:
                    emit("vector", lambda v, pY=pY, dc=dc: v.tensor_copy(
                        accf[:, dc * NT:(dc + 1) * NT], pY[:, :]),
                        ins=[bY], outs=[b_acc])
                else:
                    emit("vector", lambda v, pY=pY, dc=dc: v.tensor_add(
                        accf[:, dc * NT:(dc + 1) * NT],
                        accf[:, dc * NT:(dc + 1) * NT], pY[:, :]),
                        ins=[bY, b_acc], outs=[b_acc])

        emit("vector", lambda v, x1b=x1b: [
            v.tensor_add(accf[:, c * NT:(c + 1) * NT], accf[:, c * NT:(c + 1) * NT],
                         x1b[:, c * NT:(c + 1) * NT]) for c in range(DC)][-1],
            ins=[b_x1b], outs=[b_acc])
        record_ln_stats(accf, b_acc, sqfb, b_sqfb, rows2, b_rows2, rows2_bf, b_rows2bf)
        record_ln_apply(accf, b_acc, outf, b_outf, rows2_bf, b_rows2bf)
        emit("gpsimd", lambda g, oi=oi: g.dma_start(
            o_core.ap()[:, oi * NT:(oi + 1) * NT]
            .rearrange("(c p) n -> p c n", p=128),
            outf.rearrange("p (c n) -> p c n", n=NT)),
            ins=[b_outf], outs=[], sem="gout", n=16)

    R.op("gpsimd", None, waits=[("gout", R.ctr["gout"]), ("gy", R.ctr["gy"])])

    sems_needed = sorted(R.ctr.keys())
    return nc, R, sems_needed


_CACHED = {}


def _get_program():
    if "nc" not in _CACHED:
        nc, R, sems_needed = build_program()
        import contextlib
        stack = contextlib.ExitStack()
        sems = {name: stack.enter_context(nc.semaphore(name)) for name in sems_needed}
        block = stack.enter_context(nc.Block())
        R.emit(block, sems)
        stack.close()
        _CACHED["nc"] = nc
    return _CACHED["nc"]


def kernel(**inputs):
    x = np.asarray(inputs["L0_attn_x"])
    B, Sfull, Dm = x.shape
    assert (B, Sfull, Dm) == (4, 8192, 1024)
    x_t = np.ascontiguousarray(x.transpose(0, 2, 1))

    wk = np.ascontiguousarray(inputs["L0_attn_W_K"])
    wugg = np.ascontiguousarray(inputs["L0_attn_W_ugg"])
    wout = np.ascontiguousarray(inputs["L0_attn_W_out"])
    wr = np.ascontiguousarray(inputs["L0_ffn_W_gate_router"])
    biasr = np.ascontiguousarray(inputs["L0_ffn_expert_bias"]).reshape(1, E)
    wg_all = np.ascontiguousarray(np.concatenate(
        [inputs["L0_ffn_Wg"], inputs["L0_ffn_Wg_s"][None]], axis=0)).reshape(NE * D, F)
    wu_all = np.ascontiguousarray(np.concatenate(
        [inputs["L0_ffn_Wu"], inputs["L0_ffn_Wu_s"][None]], axis=0)).reshape(NE * D, F)
    wd_all = np.ascontiguousarray(np.concatenate(
        [inputs["L0_ffn_Wd"], inputs["L0_ffn_Wd_s"][None]], axis=0)).reshape(NE * F, D)
    rms = np.ascontiguousarray(np.asarray(inputs["L0_attn_rms_scale"]).reshape(DC, 128).T)
    split = np.ascontiguousarray(np.asarray(inputs["L0_attn_split_scale"]).reshape(DC, 128).T)
    ident = np.eye(128, dtype=np.float32)
    onehot = np.zeros((E, E * 128), dtype=ml_dtypes.bfloat16)
    for e in range(E):
        onehot[e, e * 128:(e + 1) * 128] = 1.0

    shared = dict(wk_d=wk, wugg_d=wugg, wout_d=wout, wr_d=wr, biasr_d=biasr,
                  wg_d=wg_all, wu_d=wu_all, wd_d=wd_all, rms_d=rms,
                  split_d=split, ident_d=ident, onehot_d=onehot)
    in_maps = []
    for c in range(8):
        b, h = c // 2, c % 2
        if h == 1:
            xcv = x_t[b]
        else:
            xcv = np.concatenate(
                [np.zeros((D, 4096), dtype=np.float32), x_t[b][:, 0:4096]], axis=1)
        m = {"xc": np.ascontiguousarray(xcv)}
        m.update(shared)
        in_maps.append(m)

    nc = _get_program()
    res = run_bass_kernel_spmd(nc, in_maps, core_ids=list(range(8)))
    out = np.empty((4, 8192, 1024), dtype=np.float32)
    for c in range(8):
        b, h = c // 2, c % 2
        out[b, h * 4096:(h + 1) * 4096, :] = res.results[c]["o_core"].T
    return out
